# version 13
# baseline (speedup 1.0000x reference)
import math
import os
import numpy as np
import ml_dtypes

import concourse.bass as bass
import concourse.mybir as mybir
from concourse.bass_utils import run_bass_kernel_spmd

F32 = mybir.dt.float32
BF16 = mybir.dt.bfloat16
AF = mybir.ActivationFunctionType
ALU = mybir.AluOpType
AX = mybir.AxisListType

NC = 8
S = 8192
DM = 4096
HQ = 4
OWN = 1024
TPB = 8
NQB = 16
GAMMA = 1.0 / math.sqrt(128.0)
SHIFT = -20.0
EPS = 1e-5




_CH = {"dve": 0, "act": 0, "gp": 0}
_CHSEMS = {}


def _ch(key, e, inst):
    _CH[key] += 1
    e_sem = _CHSEMS[key]
    inst.then_inc(e_sem, 1)
    e.wait_ge(e_sem, _CH[key])
    return inst


def _mk_rope(e_qtmp, e_swp, e_cs, e_tmp, dsts):
    def fn(e):
        inst = None
        for j in range(2):
            c0, c1 = j * 512, (j + 1) * 512
            tA = e_tmp[0:64, 0:512]; tB = e_tmp[0:64, 512:1024]
            tC = e_tmp[64:128, 0:512]; tD = e_tmp[64:128, 512:1024]
            e.tensor_mul(tA, e_qtmp[0:64, c0:c1], e_cs[0:64, c0:c1])
            _ch("dve", e, e.tensor_mul(tB, e_swp[0:64, c0:c1],
                                       e_cs[0:64, 1024 + c0:1024 + c1]))
            e.tensor_sub(dsts[0][:, c0:c1], tA, tB)
            e.tensor_mul(tC, e_swp[64:128, c0:c1], e_cs[64:128, c0:c1])
            _ch("dve", e, e.tensor_mul(tD, e_qtmp[64:128, c0:c1],
                                       e_cs[64:128, 1024 + c0:1024 + c1]))
            inst = e.tensor_add(dsts[1][:, c0:c1], tC, tD)
        return inst
    return fn

class Plan:

    def __init__(self):
        self.ops = []
        self.cnt = {}

    label = ""

    def emit(self, engine, fn, waits=(), inc=None):
        ws = [(s, v) for (s, v) in waits if v > 0]
        if inc is not None:
            self.cnt[inc[0]] = self.cnt.get(inc[0], 0) + inc[1]
        self.ops.append((engine, ws, fn, inc, Plan.label))

    def tot(self, sem):
        return self.cnt.get(sem, 0)


def build_program(return_plan=False):
    nc = bass.Bass("TRN2", target_bir_lowering=False,
                   detect_race_conditions=not os.environ.get("KMK_NORACE"))

    def ext(name, shape, dtype=F32):
        return nc.dram_tensor(name, shape, dtype, kind="ExternalInput").ap()

    x = ext("x", [S, DM])
    xown = ext("xown", [OWN, DM])
    wq = ext("wq", [DM, 512]); wk = ext("wk", [DM, 128]); wv = ext("wv", [DM, 128])
    bq4 = ext("bq4", [128, 4]); bk1 = ext("bk1", [128, 1]); bv1 = ext("bv1", [128, 1])
    wo = ext("wo", [DM, DM]); bo = ext("bo", [1, DM])
    wbT = ext("wbT", [DM, DM]); bb = ext("bb", [1, DM])
    wrT = ext("wrT", [DM, 8])
    aT = ext("aT", [DM, 128])
    bTm = ext("bTm", [128, DM])
    cs = ext("cs", [128, 2 * S])
    swapm_d = ext("swapm", [128, 128], F32)
    masks_d = ext("masks", [128, 4 * 512], BF16)
    identb_d = ext("identb", [128, 128], BF16)
    ident8f_d = ext("ident8f", [8, 8], F32)
    identf_d = ext("identf", [128, 128], F32)
    onescol_d = ext("onescol", [128, 1], BF16)
    onesrow_b_d = ext("onesrow_b", [1, 128], BF16)
    onesrow_f_d = ext("onesrow_f", [1, 128], F32)
    Ef_d = ext("Ef", [8, 128], F32)
    cvec_d = ext("cvec", [128, 4], F32)

    out_ext = nc.dram_tensor("out", [OWN, DM], F32, kind="ExternalOutput").ap()

    w_bf = nc.dram_tensor("w_bf", [DM, 768], BF16).ap()
    wo_bf = nc.dram_tensor("wo_bf", [DM, DM], BF16).ap()
    wb_bf = nc.dram_tensor("wb_bf", [DM, DM], BF16).ap()
    a_bf = nc.dram_tensor("a_bf", [128, DM], BF16).ap()
    r_bf = nc.dram_tensor("r_bf", [128, 256], BF16).ap()
    bT_bf = nc.dram_tensor("bT_bf", [128, DM], BF16).ap()
    ctx_send = nc.dram_tensor("ctx_send", [512, S], BF16).ap()
    gath = nc.dram_tensor("gath", [NC * 512, S], BF16, addr_space="Shared").ap()
    DBG = bool(os.environ.get("KMK_DEBUG"))
    dbgkind = "ExternalOutput" if DBG else "Internal"
    g_dram = nc.dram_tensor("g_dram", [OWN, DM], F32, kind=dbgkind).ap()
    biasrows = nc.dram_tensor("biasrows", [4, DM], BF16).ap()
    dbg = {}
    if DBG:
        dbg["xnT"] = nc.dram_tensor("dbg_xnT", [128, TPB * DM], BF16, kind="ExternalOutput").ap()
        dbg["Q"] = nc.dram_tensor("dbg_Q", [128, HQ * 2048], BF16, kind="ExternalOutput").ap()
        dbg["K"] = nc.dram_tensor("dbg_K", [128, S], BF16, kind="ExternalOutput").ap()
        dbg["V"] = nc.dram_tensor("dbg_V", [128, S], BF16, kind="ExternalOutput").ap()
        dbg["hw"] = nc.dram_tensor("dbg_hw", [128, 1024], BF16, kind="ExternalOutput").ap()
        dbg["ce"] = nc.dram_tensor("dbg_ce", [128, 1024], F32, kind="ExternalOutput").ap()
        dbg["lt"] = nc.dram_tensor("dbg_lt", [128, 64], F32, kind="ExternalOutput").ap()
        dbg["scr2"] = nc.dram_tensor("dbg_scr2", [128, 1024], F32, kind="ExternalOutput").ap()
        dbg["ctxr"] = nc.dram_tensor("dbg_ctxr", [128, 32 * 1024], BF16, kind="ExternalOutput").ap()
        dbg["send"] = nc.dram_tensor("dbg_send", [512, S], BF16, kind="ExternalOutput").ap()
        dbg["wbf"] = nc.dram_tensor("dbg_wbf", [DM, 768], BF16, kind="ExternalOutput").ap()
        dbg["qtmp"] = nc.dram_tensor("dbg_qtmp", [128, 1024], F32, kind="ExternalOutput").ap()
        dbg["ps03"] = nc.dram_tensor("dbg_ps03", [128, 2048], F32, kind="ExternalOutput").ap()
        dbg["kps"] = nc.dram_tensor("dbg_kps", [128, 1024], F32, kind="ExternalOutput").ap()
        dbg["abf"] = nc.dram_tensor("dbg_abf", [128, DM], BF16, kind="ExternalOutput").ap()

    p = Plan()

    def sbuf(name, shape, dtype):
        return nc.sbuf_tensor(name, shape, dtype).__enter__()

    Qsb = sbuf("Qsb", [128, HQ * 2048], BF16)
    Ksb = sbuf("Ksb", [128, S], BF16)
    Vsb = sbuf("Vsb", [128, S], BF16)
    identb = sbuf("identb_s", [128, 128], BF16)
    identf = sbuf("identf_s", [128, 128], F32)
    ident8f = sbuf("ident8f_s", [8, 8], F32)
    onescol = sbuf("onescol_s", [128, 1], BF16)
    onesrow_b = sbuf("onesrow_b_s", [1, 128], BF16)
    onesrow_f = sbuf("onesrow_f_s", [1, 128], F32)
    Ef = sbuf("Ef_s", [8, 128], F32)
    cvec = sbuf("cvec_s", [128, 4], F32)
    swapm = sbuf("swapm_s", [128, 128], F32)
    ctxn = sbuf("ctxn", [128, 1024], BF16)

    ARENA_B = 158912
    arena = sbuf("arena", [128, ARENA_B], mybir.dt.uint8)

    def carve(off_b, cols, dtype):
        esz = mybir.dt.size(dtype)
        return arena[:, off_b:off_b + cols * esz].bitcast(dtype)

    o = 0
    xb = [carve(o, DM, F32), carve(o + DM * 4, DM, F32)]; o += 2 * DM * 4
    xn2 = [carve(o, DM, BF16), carve(o + DM * 2, DM, BF16)]; o += DM * 4
    cs_sb = carve(o, 2048, F32); o += 8192
    qtmp = carve(o, 1024, F32); o += 4096
    ropet = carve(o, 1024, F32); o += 4096
    stt = carve(o, 32, F32); o += 128
    wsb = [carve(o + i * 1536, 768, BF16) for i in range(3)]; o += 3 * 1536
    cvst = carve(o, 1024, F32); o += 4096
    cvst_b = carve(o, 1024, BF16); o += 2048
    bqs = carve(o, 4, F32); o += 16
    bks = carve(o, 1, F32); o += 4
    bvs = carve(o, 1, F32); o += 4
    P1_LN_END = o
    XNT_OFF = 76960
    assert o <= XNT_OFF, o
    xnT = carve(XNT_OFF, TPB * DM, BF16)
    xnTv = xnT.rearrange("p (tt kk c) -> p tt kk c", tt=8, kk=32, c=128)

    def mv_xnT(k, j):
        return xnTv[:, 4 * j:4 * j + 4, k, :]

    o = 0
    wbstream = [carve(o + i * 4096, 2048, BF16) for i in range(3)]; o += 3 * 4096
    aT_sb = carve(o, DM, BF16); o += DM * 2
    bT_sb = carve(o, DM, BF16); o += DM * 2
    wrT_sb = carve(o, 256, BF16); o += 512
    ltT = carve(o, 1024, F32); o += 4096
    ctT = carve(o, 1024, F32); o += 4096
    ce = carve(o, 1024, F32); o += 4096
    ltok = carve(o - 4096, 64, F32)
    cbuf = carve(o - 4096 + 256, 64, F32)
    tp2 = carve(o - 4096 + 512, 64, F32)
    hw = carve(o, 1024, BF16); o += 2048
    bbr = carve(o, 2 * DM, BF16); o += DM * 4
    gstage = [carve(o, 2048, F32), carve(o + 8192, 2048, F32)]; o += 16384
    assert o <= XNT_OFF, o

    o = XNT_OFF + TPB * DM * 2
    PTs = carve(o, 2048, BF16); o += 4096
    mask_sb = carve(o, 4 * 512, BF16); o += 4096
    scr2 = carve(o, 1024, F32); o += 4096
    rb = carve(o, 512, F32); o += 2048
    assert o <= ARENA_B, o

    o = 0
    ctxr = carve(o, 32 * 1024, BF16); o += 32 * 1024 * 2
    wostream = [carve(o + i * 4096, 2048, BF16) for i in range(4)]; o += 4 * 4096
    ghalf = [carve(o, 2048, F32), carve(o + 8192, 2048, F32)]; o += 16384
    xohalf = [carve(o, 2048, F32), carve(o + 8192, 2048, F32)]; o += 16384
    ohalf = [carve(o, 2048, F32), carve(o + 8192, 2048, F32)]; o += 16384
    bor = carve(o, 2 * DM, BF16); o += DM * 4
    assert o <= ARENA_B

    PS = nc.psum_tensor("PS", [128, 4096], F32).__enter__()

    def bank(b, w=1):
        return PS[:, b * 512:(b + w) * 512]

    def bankb16(b):
        return PS[:, b * 512:(b + 1) * 512].bitcast(BF16)

    SHIFT_AP = cvec[:, 0:1]; GAMMA_AP = cvec[:, 1:2]; EPS_AP = cvec[:, 2:3]

    pre_list = [
        (cvec[:, :], cvec_d), (identb[:, :], identb_d), (identf[:, :], identf_d),
        (ident8f[:, :], ident8f_d), (onescol[:, :], onescol_d),
        (onesrow_b[:, :], onesrow_b_d), (onesrow_f[:, :], onesrow_f_d),
        (Ef[:, :], Ef_d), (swapm[:, :], swapm_d),
        (bqs[:, :], bq4), (bks[:, :], bk1), (bvs[:, :], bv1),
    ]
    for dst, src in pre_list:
        p.emit('sp', (lambda dst=dst, src=src: lambda e: e.dma_start(out=dst, in_=src))(),
               inc=("s_pre", 16))
    PRE_T = p.tot("s_pre")

    for (srcrow, r0) in ((bo, 0), (bb, 2)):
        p.emit('sp', (lambda srcrow=srcrow: lambda e: e.dma_start(
            out=xb[0][0:1, :], in_=srcrow))(),
            waits=[("s_brow", p.tot("s_brow"))], inc=("s_browh", 16))
        p.emit('gp', lambda e: e.tensor_copy(xn2[0][0:1, :], xb[0][0:1, :]),
               waits=[("s_browh", p.tot("s_browh")),
                      ("s_browd", p.tot("s_browd"))], inc=("s_brow", 1))
        p.emit('gp', (lambda r0=r0: lambda e: e.dma_start(
            out=biasrows[r0:r0 + 1, :], in_=xn2[0][0:1, :]))(),
            waits=[("s_brow", p.tot("s_brow"))], inc=("s_browd", 16))
        p.emit('dve', lambda e: e.tensor_sub(xb[1][0:1, :], xb[0][0:1, :], xn2[0][0:1, :]),
               waits=[("s_brow", p.tot("s_brow")), ("s_browd", p.tot("s_browd"))],
               inc=("s_brow", 1))
        p.emit('gp', lambda e: e.tensor_copy(xn2[0][0:1, :], xb[1][0:1, :]),
               waits=[("s_brow", p.tot("s_brow")), ("s_browd", p.tot("s_browd"))],
               inc=("s_brow", 1))
        p.emit('gp', (lambda r0=r0: lambda e: e.dma_start(
            out=biasrows[r0 + 1:r0 + 2, :], in_=xn2[0][0:1, :]))(),
            waits=[("s_brow", p.tot("s_brow"))], inc=("s_browd", 16))
    T_BROW = p.tot("s_brow")
    T_BROWD = p.tot("s_browd")

    conv_jobs = [("wqkv", k) for k in range(32)]
    conv_jobs += [("aT", c) for c in range(4)] + [("rT", 0)] + [("bT", c) for c in range(4)]
    conv_jobs += [("wb", (k, c)) for k in range(32) for c in range(4)]
    conv_jobs += [("wo", (k, c)) for k in range(32) for c in range(4)]
    N_WQKV_JOBS = 32

    cstate = {"idx": 0, "wqkv_out_t": None, "ffn_out_t": None}

    def conv_aps(job):
        kind, arg = job
        if kind == "wqkv":
            k = arg
            return [(wq[k * 128:(k + 1) * 128, :], 0, 512),
                    (wk[k * 128:(k + 1) * 128, :], 512, 128),
                    (wv[k * 128:(k + 1) * 128, :], 640, 128)], \
                w_bf[k * 128:(k + 1) * 128, :], 768
        if kind == "wo":
            k, c = arg
            return [(wo[k * 128:(k + 1) * 128, c * 1024:(c + 1) * 1024], 0, 1024)], \
                wo_bf[k * 128:(k + 1) * 128, c * 1024:(c + 1) * 1024], 1024
        if kind == "wb":
            k, c = arg
            return [(wbT[k * 128:(k + 1) * 128, c * 1024:(c + 1) * 1024], 0, 1024)], \
                wb_bf[k * 128:(k + 1) * 128, c * 1024:(c + 1) * 1024], 1024
        if kind == "aT":
            c = arg
            srcv = aT.rearrange("(a p) c -> p a c", p=128)
            return [(srcv[:, c * 8:(c + 1) * 8, :], 0, 1024)], \
                a_bf[:, c * 1024:(c + 1) * 1024], 1024
        if kind == "rT":
            srcv = wrT.rearrange("(a p) c -> p a c", p=128)
            return [(srcv, 0, 256)], r_bf[:, :], 256
        if kind == "bT":
            c = arg
            return [(bTm[:, c * 1024:(c + 1) * 1024], 0, 1024)], \
                bT_bf[:, c * 1024:(c + 1) * 1024], 1024

    def emit_conv_job():
        if cstate["idx"] >= len(conv_jobs):
            return
        job = conv_jobs[cstate["idx"]]
        cu = cstate["idx"]
        cstate["idx"] += 1
        segs, dst, width = conv_aps(job)
        for (sap, off, w) in segs:
            p.emit('sp', (lambda sap=sap, off=off, w=w:
                          lambda e: e.dma_start(out=cvst[:, off:off + w], in_=sap))(),
                   waits=[("s_cv", cu)],
                   inc=("s_cvin0", 16))
        t_in = p.tot("s_cvin0")
        ceng = 'gp' if cu % 2 == 0 else 'dve'
        p.emit(ceng, (lambda width=width:
                      lambda e: e.tensor_copy(cvst_b[:, 0:width],
                                              cvst[:, 0:width]))(),
               waits=[("s_cvin0", t_in), ("s_cvout0", p.tot("s_cvout0"))],
               inc=("s_cv", 1))
        p.emit('sp', (lambda dst=dst, width=width:
                      lambda e: e.dma_start(out=dst, in_=cvst_b[:, 0:width]))(),
               waits=[("s_cv", p.tot("s_cv"))], inc=("s_cvout0", 16))
        if cstate["idx"] == N_WQKV_JOBS:
            cstate["wqkv_out_t"] = (p.tot("s_cvout0"), 0)
        if cstate["idx"] == 169:
            cstate["ffn_out_t"] = p.tot("s_cvout0")

    def trickle(n):
        for _ in range(n):
            emit_conv_job()

    if os.environ.get("KMK_CONVFIRST"):
        while cstate["idx"] < len(conv_jobs):
            emit_conv_job()

    NT = 72
    woff = {0: 0, 1: 128, 2: 256, 3: 384, 4: 512, 5: 640}
    PHASE1_ONLY = os.environ.get("KMK_PHASE") == "1"
    att_stmm = {}

    p.emit('sp', lambda e: e.dma_start(out=mask_sb, in_=masks_d),
           inc=("s_msk0", 16))

    def emit_tile(t):
        b = t // TPB
        own = (b == 8)
        src = (xown[(t - 64) * 128:(t - 63) * 128, :] if own
               else x[t * 128:(t + 1) * 128, :])
        xslot = xb[t % 2]
        xsem = "s_x%d" % (t % 2)
        Plan.label = f"tile{t}"
        p.emit('sp', (lambda src=src, xslot=xslot: lambda e: e.dma_start(out=xslot, in_=src))(),
               waits=[("s_xn", max(0, t - 1)), ("s_brow", T_BROW if t < 2 else 0),
                      ("s_browd", T_BROWD if t < 2 else 0)],
               inc=(xsem, 16))
        t_x = p.tot(xsem)
        if not own and t % TPB == 0:
            p.emit('sp', (lambda b=b: lambda e: e.dma_start(
                out=cs_sb[:, 0:1024], in_=cs[:, b * 1024:(b + 1) * 1024]))(),
                waits=[("s_qe", p.tot("s_qe"))], inc=("s_cs", 16))
            p.emit('sp', (lambda b=b: lambda e: e.dma_start(
                out=cs_sb[:, 1024:2048], in_=cs[:, S + b * 1024:S + (b + 1) * 1024]))(),
                inc=("s_cs", 16))
        trickle(4)

        p.emit('dve', (lambda xslot=xslot, sb_=(t % 2) * 16: lambda e: e.reduce_sum(
            stt[:, sb_:sb_ + 1], xslot, axis=AX.X))(),
            waits=[(xsem, t_x), ("s_xn", max(0, t - 1)), ("s_stats", max(0, t - 1)),
                   ("s_pre", PRE_T if t < 2 else 0)],
            inc=("s_dsum", 1))
        xn_sl = xn2[t % 2]
        sb_ = (t % 2) * 16
        p.emit('act', (lambda xslot=xslot, xn_sl=xn_sl, sb_=sb_: lambda e: e.activation(
            xn_sl[:, :], xslot, AF.Square, accum_out=stt[:, sb_ + 1:sb_ + 2]))(),
            waits=[(xsem, t_x), ("s_tp", max(0, 4 * (t - 1))),
                   ("s_browd", T_BROWD if t < 2 else 0),
                   ("s_vtp", (t - 1) // 8 if t % TPB == 1 else 0),
                   ("s_pre", PRE_T if t < 2 else 0)], inc=("s_sq", 1))

        def stats1(e, b_=sb_):
            _ch("dve", e, e.tensor_scalar_mul(stt[:, b_ + 2:b_ + 3], stt[:, b_:b_ + 1],
                                              1.0 / 4096.0))
            e.tensor_scalar_mul(stt[:, b_ + 3:b_ + 4], stt[:, b_ + 1:b_ + 2], 1.0 / 4096.0)
            _ch("dve", e, e.tensor_mul(stt[:, b_:b_ + 1], stt[:, b_ + 2:b_ + 3],
                                       stt[:, b_ + 2:b_ + 3]))
            return e.tensor_sub(stt[:, b_ + 3:b_ + 4], stt[:, b_ + 3:b_ + 4],
                                stt[:, b_:b_ + 1])
        p.emit('dve', stats1, waits=[("s_sq", t + 1), ("s_dsum", t + 1)], inc=("s_v1", 1))
        p.emit('act', (lambda b_=sb_: lambda e: e.activation(
            stt[:, b_ + 4:b_ + 5], stt[:, b_ + 3:b_ + 4], AF.Sqrt, bias=EPS_AP))(),
               waits=[("s_v1", t + 1)], inc=("s_v2", 1))

        def stats2(e, b_=sb_):
            _ch("dve", e, e.reciprocal(stt[:, b_ + 5:b_ + 6], stt[:, b_ + 4:b_ + 5]))
            return e.tensor_scalar(stt[:, b_ + 6:b_ + 7], stt[:, b_ + 2:b_ + 3],
                                   stt[:, b_ + 5:b_ + 6], -1.0, ALU.mult, ALU.mult)
        p.emit('dve', stats2, waits=[("s_v2", t + 1)], inc=("s_stats", 1))
        p.emit('dve', (lambda xslot=xslot, xn_sl=xn_sl, b_=sb_: lambda e: e.tensor_scalar(
            xn_sl[:, :], xslot, stt[:, b_ + 5:b_ + 6], stt[:, b_ + 6:b_ + 7],
            ALU.mult, ALU.add))(),
            waits=[("s_stats", t + 1)], inc=("s_xn", 1))

        for beta in range(4):
            tb = 6 + (beta % 2)

            def mk_tp(beta=beta, tb=tb, xn_sl=xn_sl):
                def fn(e):
                    inst = None
                    for j in range(8):
                        k = beta * 8 + j
                        inst = e.matmul(bankb16(tb)[:, j * 128:(j + 1) * 128],
                                        xn_sl[:, k * 128:(k + 1) * 128], identb[:, :],
                                        is_transpose=True,
                                        start=(j == 0), stop=(j == 7))
                    return inst
                return fn
            p.emit('pe', mk_tp(),
                   waits=[("s_xn", t + 1), ("s_pre", PRE_T),
                          ("s_xnTe", max(0, 4 * t + beta - 1)),
                          ("s_ve", t // TPB),
                          ("s_rec", p.tot("s_rec")), ("s_rbe", p.tot("s_rbe"))],
                   inc=("s_tp", 1))
            tau = t % TPB
            dst = xnT[:, tau * 4096 + beta * 1024: tau * 4096 + (beta + 1) * 1024]
            p.emit('act', (lambda dst=dst, tb=tb: lambda e: e.activation(
                dst, bankb16(tb), AF.Identity))(),
                   waits=[("s_tp", 4 * t + beta + 1)], inc=("s_xnTe", 1))

    def emit_qkv(b):
        Plan.label = f"qkv_b{b}"
        t_xnTe = p.tot("s_xnTe")
        t_cs = p.tot("s_cs")
        war_stmm = att_stmm.get(2 * b - 3, 0)
        for oset in range(2):
            ocs = [0, 1, 2] if oset == 0 else [3, 4, 5]
            q0 = p.tot("s_qe"); v0 = p.tot("s_ve")
            for k in range(32):
                slot = k % 3
                wsem = "s_wst%d" % slot
                p.emit('sp', (lambda k=k, slot=slot: lambda e: e.dma_start(
                    out=wsb[slot], in_=w_bf[k * 128:(k + 1) * 128, :]))(),
                    waits=[("s_cvout0", cstate["wqkv_out_t"][0] if cstate["wqkv_out_t"] else 0),
                           ("s_wsb", p.tot("s_wsb") - (0 if k < 3 else 2))],
                    inc=(wsem, 16))
                t_wst = p.tot(wsem)

                def mk_qkv(k=k, slot=slot, ocs=ocs):
                    def fn(e):
                        inst = None
                        for si, oc in enumerate(ocs):
                            lhsT = wsb[slot][:, woff[oc]:woff[oc] + 128]
                            for j in range(2):
                                inst = e.matmul(bank(si * 2 + j), lhsT, mv_xnT(k, j),
                                                start=(k == 0), stop=(k == 31))
                        return inst
                    return fn
                wl = [(wsem, t_wst)]
                if k == 0:
                    wl += [("s_xnTe", t_xnTe), ("s_qe", q0), ("s_ve", v0),
                           ("s_cs", t_cs), ("s_cn", p.tot("s_cn")),
                           ("s_exp", p.tot("s_exp"))]
                p.emit('pe', mk_qkv(), waits=wl, inc=("s_wsb", 1))
            t_done = p.tot("s_wsb")

            for si, oc in enumerate(ocs):
                Plan.label = f"evac_b{b}_oc{oc}"
                pw = bank(si * 2, 2)
                if oc <= 3:
                    h = oc
                    qbase = (b % 2) * 4096 + h * 1024
                    dsts = [Qsb[0:64, qbase:qbase + 1024],
                            Qsb[64:128, qbase:qbase + 1024]]
                    p.emit('dve', (lambda pw=pw, h=h: lambda e: e.tensor_scalar_add(
                        qtmp[:, :], pw, bqs[:, h:h + 1]))(),
                        waits=[("s_wsb", t_done), ("s_cs", t_cs),
                               ("s_qe", p.tot("s_qe"))], inc=("s_qb", 1))
                    p.emit('pe', (lambda si=si: lambda e: [
                        e.matmul(bank(si * 2), swapm[:, :], qtmp[:, 0:512],
                                 start=True, stop=True),
                        e.matmul(bank(si * 2 + 1), swapm[:, :], qtmp[:, 512:1024],
                                 start=True, stop=True)][-1])(),
                        waits=[("s_qb", p.tot("s_qb"))], inc=("s_qsw", 1))
                    p.emit('dve', (lambda pw=pw, dsts=dsts: _mk_rope(
                        e_qtmp=qtmp, e_swp=pw, e_cs=cs_sb, e_tmp=ropet,
                        dsts=dsts))(),
                        waits=[("s_qsw", p.tot("s_qsw")),
                               ("s_stmm", war_stmm)], inc=("s_qe", 1))
                elif oc == 4:
                    dsts = [Ksb[0:64, b * 1024:(b + 1) * 1024],
                            Ksb[64:128, b * 1024:(b + 1) * 1024]]

                    p.emit('dve', (lambda pw=pw: lambda e: e.tensor_scalar_add(
                        qtmp[:, :], pw, bks[:, 0:1]))(),
                        waits=[("s_wsb", t_done), ("s_cs", t_cs),
                               ("s_qe", p.tot("s_qe"))], inc=("s_qb", 1))

                    p.emit('pe', (lambda si=si: lambda e: [
                        e.matmul(bank(si * 2), swapm[:, :], qtmp[:, 0:512],
                                 start=True, stop=True),
                        e.matmul(bank(si * 2 + 1), swapm[:, :], qtmp[:, 512:1024],
                                 start=True, stop=True)][-1])(),
                        waits=[("s_qb", p.tot("s_qb"))], inc=("s_qsw", 1))
                    p.emit('dve', (lambda pw=pw, dsts=dsts: _mk_rope(
                        e_qtmp=qtmp, e_swp=pw, e_cs=cs_sb, e_tmp=ropet,
                        dsts=dsts))(),
                        waits=[("s_qsw", p.tot("s_qsw"))], inc=("s_qe", 1))
                else:
                    vsl = xn2[1][:, 0:1024]
                    p.emit('act', (lambda pw=pw, vsl=vsl: lambda e: e.activation(
                        vsl, pw, AF.Identity, bias=bvs[:, 0:1]))(),
                        waits=[("s_wsb", t_done), ("s_vtp", p.tot("s_vtp"))],
                        inc=("s_vs", 1))

                    def mk_vtp(vsl=vsl):
                        def fn(e):
                            inst = None
                            for j in range(8):
                                inst = e.matmul(
                                    bankb16(6)[:, j * 128:(j + 1) * 128],
                                    vsl[:, j * 128:(j + 1) * 128], identb[:, :],
                                    is_transpose=True, start=(j == 0), stop=(j == 7))
                            return inst
                        return fn
                    p.emit('pe', mk_vtp(), waits=[("s_vs", p.tot("s_vs")),
                                                  ("s_qe", p.tot("s_qe")),
                                                  ("s_rec", p.tot("s_rec")),
                                                  ("s_xnTe", p.tot("s_xnTe"))],
                           inc=("s_vtp", 1))
                    p.emit('dve', (lambda b=b: lambda e: e.tensor_copy(
                        Vsb[:, b * 1024:(b + 1) * 1024], bankb16(6)))(),
                        waits=[("s_vtp", p.tot("s_vtp"))], inc=("s_ve", 1))

    def emit_attn(qb):
        if PHASE1_ONLY:
            att_stmm[qb] = p.tot("s_stmm")
            return
        b = qb // 2
        Plan.label = f"attn{qb}"
        kcmax = 4 * (qb + 1)
        qoff = (b % 2) * 4096 + (qb % 2) * 512
        t_xnTe0 = p.tot("s_xnTe"); t_ve0 = p.tot("s_ve")
        for hp in range(2):
            heads = (hp * 2, hp * 2 + 1)
            base_cmm = p.tot("s_cmm")
            base_stmm = p.tot("s_stmm")
            base_exp = p.tot("s_exp")
            for it in range(kcmax + 1):
                if it < kcmax:
                    kc = it
                    slot = kc % 2
                    sb0 = 2 + slot * 2

                    def mk_st(kc=kc, sb0=sb0, heads=heads, qoff=qoff):
                        def fn(e):
                            inst = None
                            for i, h in enumerate(heads):
                                inst = e.matmul(
                                    bank(sb0 + i),
                                    Ksb[:, kc * 128:(kc + 1) * 128],
                                    Qsb[:, qoff + h * 1024: qoff + h * 1024 + 512],
                                    start=True, stop=True)
                            return inst
                        return fn
                    expw = base_exp + (kc - 1) if kc >= 2 else base_exp
                    wl = [("s_exp", max(0, expw))]
                    if kc == 0 and hp == 0:
                        wl += [("s_qe", p.tot("s_qe")), ("s_ve", p.tot("s_ve")),
                               ("s_vs", p.tot("s_vs"))]
                        if qb == 0:
                            wl.append(("s_msk0", 16))
                    p.emit('pe', mk_st(), waits=wl, inc=("s_stmm", 1))
                if it >= 1:
                    kcc = it - 1
                    slot = kcc % 2
                    sb0 = 2 + slot * 2
                    diag = kcc >= kcmax - 4
                    expwl = [("s_stmm", base_stmm + kcc + 1)]
                    if kcc >= 2:
                        expwl += [("s_cmm", base_cmm + kcc - 1)]
                    p.emit('act', (lambda sb0=sb0, slot=slot:
                                   lambda e: e.activation(
                                       PTs[:, slot * 1024:(slot + 1) * 1024],
                                       bank(sb0, 2), AF.Exp,
                                       bias=SHIFT_AP, scale=GAMMA_AP))(),
                           waits=expwl, inc=("s_exp", 1))
                    t_exp = p.tot("s_exp")
                    if diag:
                        c = kcc - (kcmax - 4)

                        def mk_msk(slot=slot, c=c):
                            def fn(e):
                                e.tensor_mul(
                                    PTs[:, slot * 1024:slot * 1024 + 512],
                                    PTs[:, slot * 1024:slot * 1024 + 512],
                                    mask_sb[:, c * 512:(c + 1) * 512])
                                return e.tensor_mul(
                                    PTs[:, slot * 1024 + 512:slot * 1024 + 1024],
                                    PTs[:, slot * 1024 + 512:slot * 1024 + 1024],
                                    mask_sb[:, c * 512:(c + 1) * 512])
                            return fn
                        p.emit('dve', mk_msk(), waits=[("s_exp", t_exp)],
                               inc=("s_msk", 1))
                    t_msk = p.tot("s_msk")

                    def mk_ctx(kcc=kcc, slot=slot, kcmax=kcmax):
                        def fn(e):
                            st = (kcc == 0); ed = (kcc == kcmax - 1)
                            for i in range(2):
                                e.matmul(
                                    bank(i), Vsb[:, kcc * 128:(kcc + 1) * 128],
                                    PTs[:, slot * 1024 + i * 512:
                                        slot * 1024 + (i + 1) * 512],
                                    start=st, stop=ed)
                            e.matmul(PS[0:1, 6 * 512:7 * 512], onescol[:, :],
                                     PTs[:, slot * 1024:slot * 1024 + 512],
                                     start=st, stop=ed)
                            return e.matmul(PS[0:1, 7 * 512:8 * 512], onescol[:, :],
                                            PTs[:, slot * 1024 + 512:slot * 1024 + 1024],
                                            start=st, stop=ed)
                        return fn
                    wl = [("s_exp", t_exp)]
                    if diag:
                        wl.append(("s_msk", t_msk))
                    if kcc == 0:
                        wl += [("s_cn", p.tot("s_cn")),
                               ("s_xnTe", t_xnTe0), ("s_ve", t_ve0)]
                    p.emit('pe', mk_ctx(), waits=wl, inc=("s_cmm", 1))
            t_cmm = p.tot("s_cmm")

            def mk_dev(e):
                e.reciprocal(scr2[0:1, 0:512], PS[0:1, 6 * 512:7 * 512])
                return e.reciprocal(scr2[0:1, 512:1024], PS[0:1, 7 * 512:8 * 512])
            p.emit('dve', mk_dev, waits=[("s_cmm", t_cmm)],
                   inc=("s_rec", 1))
            t_rec = p.tot("s_rec")
            for i, h in enumerate(heads):
                p.emit('pe', (lambda i=i: lambda e: e.matmul(
                    bank(7), onesrow_f[:, :],
                    scr2[0:1, i * 512:(i + 1) * 512],
                    start=True, stop=True))(),
                    waits=[("s_rec", t_rec), ("s_rbe", p.tot("s_rbe"))],
                    inc=("s_bc", 1))
                p.emit('dve', lambda e: e.tensor_copy(rb[:, :], bank(7)),
                       waits=[("s_bc", p.tot("s_bc")),
                              ("s_cn", p.tot("s_cn"))], inc=("s_rbe", 1))
                cslot = p.tot("s_cn") % 2
                ssem = "s_send%d" % cslot
                p.emit('dve', (lambda i=i, cslot=cslot: lambda e: e.tensor_mul(
                    ctxn[:, cslot * 512:(cslot + 1) * 512], bank(i), rb[:, :]))(),
                    waits=[("s_rbe", p.tot("s_rbe")),
                           (ssem, p.tot(ssem))],
                    inc=("s_cn", 1))
                p.emit('gp', (lambda h=h, qb=qb, cslot=cslot: lambda e: e.dma_start(
                    out=ctx_send[h * 128:(h + 1) * 128, qb * 512:(qb + 1) * 512],
                    in_=ctxn[:, cslot * 512:(cslot + 1) * 512]))(),
                    waits=[("s_cn", p.tot("s_cn"))], inc=(ssem, 16))
        att_stmm[qb] = p.tot("s_stmm")

    for t in range(8):
        emit_tile(t)
    emit_qkv(0)
    for b in range(8):
        nxt = 8 * (b + 1)
        emit_tile(nxt)
        emit_attn(2 * b)
        emit_tile(nxt + 1); emit_tile(nxt + 2)
        emit_tile(nxt + 3); emit_tile(nxt + 4)
        emit_attn(2 * b + 1)
        emit_tile(nxt + 5); emit_tile(nxt + 6); emit_tile(nxt + 7)
        if b < 7:
            emit_qkv(b + 1)

    assert cstate["ffn_out_t"] is not None
    FFN_CONV_T = cstate["ffn_out_t"]
    while cstate["idx"] < len(conv_jobs):
        emit_conv_job()
    CONV_OUT_T0 = p.tot("s_cvout0"); CONV_OUT_T1 = p.tot("s_cvout1")
    T_CVIN_ALL = p.tot("s_cvin0")
    T_XN_ALL = p.tot("s_xn")
    T_QE_ALL = p.tot("s_qe")
    T_VE_ALL = p.tot("s_ve")
    T_XNTE_ALL = p.tot("s_xnTe")

    p.emit('sp', lambda e: e.dma_start(out=aT_sb, in_=a_bf[:, :]),
           waits=[("s_cvout0", FFN_CONV_T),
                  ("s_qe", T_QE_ALL), ("s_xn", T_XN_ALL),
                  ("s_xnTe", T_XNTE_ALL),
                  ("s_tp", p.tot("s_tp"))],
           inc=("s_wbs", 16))
    p.emit('sp', lambda e: e.dma_start(out=bT_sb, in_=bT_bf[:, :]), inc=("s_wbs", 16))
    p.emit('sp', lambda e: e.dma_start(out=wrT_sb, in_=r_bf[:, :]), inc=("s_wbs", 16))
    p.emit('sp', lambda e: e.dma_start(out=bbr[0:1, 0:DM], in_=biasrows[2:3, :]),
           waits=[("s_browd", T_BROWD)], inc=("s_wbs", 16))
    p.emit('sp', lambda e: e.dma_start(out=bbr[0:1, DM:2 * DM], in_=biasrows[3:4, :]),
           inc=("s_wbs", 16))
    T_FFNW = p.tot("s_wbs")

    def mk_router(e):
        inst = None
        for k in range(32):
            for j in range(2):
                inst = e.matmul(PS[0:8, j * 512:(j + 1) * 512],
                                wrT_sb[:, k * 8:(k + 1) * 8], mv_xnT(k, j),
                                start=(k == 0), stop=(k == 31))
        return inst
    p.emit('pe', mk_router, waits=[("s_wbs", T_FFNW), ("s_xnTe", T_XNTE_ALL),
                                   ("s_qe", T_QE_ALL), ("s_ve", T_VE_ALL),
                                   ("s_cn", p.tot("s_cn")),
                                   ("s_exp", p.tot("s_exp"))],
           inc=("s_r", 1))
    p.emit('dve', lambda e: e.tensor_copy(ltT[0:8, :], PS[0:8, 0:1024]),
           waits=[("s_r", p.tot("s_r"))], inc=("s_re", 1))

    def mk_ltp(e):
        inst = None
        for tau in range(8):
            inst = e.matmul(bank(2)[0:128, tau * 8:(tau + 1) * 8],
                            ltT[0:8, tau * 128:(tau + 1) * 128], ident8f[:, :],
                            is_transpose=True, start=(tau == 0), stop=(tau == 7))
        return inst
    p.emit('pe', mk_ltp, waits=[("s_re", p.tot("s_re"))], inc=("s_rtp", 1))
    p.emit('dve', lambda e: e.tensor_copy(ltok[:, :], bank(2)[:, 0:64]),
           waits=[("s_rtp", p.tot("s_rtp"))], inc=("s_rte", 1))
    T_RTE = p.tot("s_rte")

    for tau in range(8):
        lt = ltok[:, tau * 8:(tau + 1) * 8]

        def mk_top2a(lt=lt):
            def fn(e):
                _ch("dve", e, e.reduce_max(tp2[:, 0:1], lt, axis=AX.X, negate=True))
                _ch("dve", e, e.tensor_scalar_mul(tp2[:, 1:2], tp2[:, 0:1], -1.0))
                _ch("dve", e, e.tensor_single_scalar(tp2[:, 8:16], lt, tp2[:, 1:2],
                                                     ALU.is_equal))
                _ch("dve", e, e.scalar_tensor_tensor(tp2[:, 16:24], tp2[:, 8:16],
                                                     -1e30, lt, ALU.mult, ALU.add))
                _ch("dve", e, e.reduce_max(tp2[:, 2:3], tp2[:, 16:24], axis=AX.X,
                                           negate=True))
                return e.tensor_scalar_mul(tp2[:, 3:4], tp2[:, 2:3], -1.0)
            return fn
        p.emit('dve', mk_top2a(), waits=[("s_rte", T_RTE), ("s_t2", p.tot("s_t2"))],
               inc=("s_t2", 1))
        p.emit('act', (lambda lt=lt: lambda e: e.activation(
            tp2[:, 24:32], lt, AF.Exp, bias=tp2[:, 0:1]))(),
            waits=[("s_t2", p.tot("s_t2"))], inc=("s_t2", 1))

        def mk_top2b(lt=lt, tau=tau):
            def fn(e):
                _ch("dve", e, e.tensor_single_scalar(tp2[:, 32:40], lt, tp2[:, 3:4],
                                                     ALU.is_ge))
                _ch("dve", e, e.tensor_mul(tp2[:, 24:32], tp2[:, 24:32], tp2[:, 32:40]))
                _ch("dve", e, e.reduce_sum(tp2[:, 4:5], tp2[:, 24:32], axis=AX.X))
                _ch("dve", e, e.reciprocal(tp2[:, 5:6], tp2[:, 4:5]))
                return e.tensor_scalar(cbuf[:, tau * 8:(tau + 1) * 8], tp2[:, 24:32],
                                       tp2[:, 5:6], 2.0, ALU.mult, ALU.mult)
            return fn
        p.emit('dve', mk_top2b(), waits=[("s_t2", p.tot("s_t2"))], inc=("s_t2", 1))
    T_T2 = p.tot("s_t2")

    def mk_ctp(e):
        inst = None
        for tau in range(8):
            bknum = 3 + tau // 4
            inst = e.matmul(PS[0:8, bknum * 512 + (tau % 4) * 128:
                               bknum * 512 + (tau % 4 + 1) * 128],
                            cbuf[:, tau * 8:(tau + 1) * 8], identf[:, :],
                            is_transpose=True, start=(tau % 4 == 0), stop=(tau % 4 == 3))
        return inst
    p.emit('pe', mk_ctp, waits=[("s_t2", T_T2)], inc=("s_ct", 1))
    p.emit('dve', lambda e: e.tensor_copy(ctT[0:8, 0:512], PS[0:8, 3 * 512:4 * 512]),
           waits=[("s_ct", p.tot("s_ct"))], inc=("s_cte", 1))
    p.emit('dve', lambda e: e.tensor_copy(ctT[0:8, 512:1024], PS[0:8, 4 * 512:5 * 512]),
           inc=("s_cte", 1))
    T_CTE = p.tot("s_cte")

    def mk_exp_c(e):
        e.matmul(bank(5), Ef[:, :], ctT[0:8, 0:512], start=True, stop=True)
        return e.matmul(bank(2), Ef[:, :], ctT[0:8, 512:1024], start=True, stop=True)
    p.emit('pe', mk_exp_c, waits=[("s_cte", T_CTE), ("s_rte", T_RTE)], inc=("s_cx", 1))
    p.emit('dve', lambda e: e.tensor_copy(ce[:, 0:512], bank(5)),
           waits=[("s_cx", p.tot("s_cx"))], inc=("s_cxe", 1))
    p.emit('dve', lambda e: e.tensor_copy(ce[:, 512:1024], bank(2)), inc=("s_cxe", 1))
    T_CXE = p.tot("s_cxe")

    def mk_h(e):
        inst = None
        for k in range(32):
            for j in range(2):
                inst = e.matmul(bank(6 + j), aT_sb[:, k * 128:(k + 1) * 128],
                                mv_xnT(k, j), start=(k == 0), stop=(k == 31))
        return inst
    p.emit('pe', mk_h, waits=[("s_wbs", T_FFNW), ("s_xnTe", T_XNTE_ALL),
                              ("s_ve", T_VE_ALL), ("s_rec", p.tot("s_rec")),
                              ("s_rbe", p.tot("s_rbe"))], inc=("s_h", 1))
    p.emit('dve', lambda e: e.tensor_mul(hw[:, :], bank(6, 2), ce[:, :]),
           waits=[("s_h", p.tot("s_h")), ("s_cxe", T_CXE)], inc=("s_hw", 1))
    T_HW = p.tot("s_hw")

    for tp_i in range(4):
        for half in range(2):
            for k in range(33):
                if k < 32:
                    slot = k % 3
                    wsem = "s_wst%d" % slot
                    p.emit('sp', (lambda k=k, half=half, slot=slot: lambda e: e.dma_start(
                        out=wbstream[slot],
                        in_=wb_bf[k * 128:(k + 1) * 128, half * 2048:(half + 1) * 2048]))(),
                        waits=[("s_base", p.tot("s_base") - (0 if k < 3 else 2)),
                               ("s_cvout0", FFN_CONV_T)],
                        inc=(wsem, 16))
                    t_w = p.tot(wsem)

                def mk_base(k=k, tp_i=tp_i, half=half):
                    def fn(e):
                        inst = None
                        for ti in range(2):
                            tau = tp_i * 2 + ti
                            if k < 32:
                                lhsT = xnT[:, tau * 4096 + k * 128:
                                           tau * 4096 + (k + 1) * 128]
                                mvr = wbstream[k % 3]
                            else:
                                lhsT = hw[:, tau * 128:(tau + 1) * 128]
                                mvr = bT_sb[:, half * 2048:(half + 1) * 2048]
                            for j in range(4):
                                inst = e.matmul(bank(ti * 4 + j), lhsT,
                                                mvr[:, j * 512:(j + 1) * 512],
                                                start=(k == 0), stop=False)
                        return inst
                    return fn
                wl = [(wsem, t_w)] if k < 32 else [("s_hw", T_HW)]
                if k == 0:
                    wl += [("s_hw", T_HW), ("s_cxe", T_CXE), ("s_cte", T_CTE),
                           ("s_re", p.tot("s_re")), ("s_rte", T_RTE),
                           ("s_g", p.tot("s_g"))]
                p.emit('pe', mk_base(), waits=wl,
                       inc=("s_base", 1) if k < 32 else None)

            def mk_bbias():
                def fn(e):
                    inst = None
                    for ti in range(2):
                        for j in range(4):
                            col = half * 2048 + j * 512
                            e.matmul(bank(ti * 4 + j), onesrow_b[:, :],
                                     bbr[0:1, col:col + 512], start=False, stop=False)
                            inst = e.matmul(bank(ti * 4 + j), onesrow_b[:, :],
                                            bbr[0:1, DM + col:DM + col + 512],
                                            start=False, stop=True)
                    return inst
                return fn
            p.emit('pe', mk_bbias(), waits=[("s_wbs", T_FFNW)], inc=("s_bmm", 1))
            t_bs = p.tot("s_bmm")
            for ti in range(2):
                tau = tp_i * 2 + ti
                gslot = gstage[ti]
                gsem = "s_gw%d" % ti
                p.emit('act', (lambda ti=ti, gslot=gslot: lambda e: e.activation(
                    gslot, bank(ti * 4, 4), AF.Silu))(),
                    waits=[("s_bmm", t_bs), (gsem, p.tot(gsem)),
                           ("s_cvout0", CONV_OUT_T0),
                           ("s_cvin0", T_CVIN_ALL)],
                    inc=("s_g", 1))
                p.emit('gp', (lambda tau=tau, half=half, gslot=gslot: lambda e: e.dma_start(
                    out=g_dram[tau * 128:(tau + 1) * 128, half * 2048:(half + 1) * 2048],
                    in_=gslot))(),
                    waits=[("s_g", p.tot("s_g"))], inc=(gsem, 16))
    T_G_ALL0 = p.tot("s_gw0"); T_G_ALL1 = p.tot("s_gw1")
    T_SILU = p.tot("s_g")
    T_BMM = p.tot("s_bmm")

    if DBG:
        p.emit('gp', lambda e: e.dma_start(out=dbg["wbf"], in_=w_bf),
               waits=[("s_cvout0", CONV_OUT_T0), ("s_cvout1", CONV_OUT_T1)],
               inc=("s_dbg", 16))
        p.emit('gp', lambda e: e.dma_start(out=dbg["abf"], in_=a_bf),
               waits=[("s_cvout0", CONV_OUT_T0), ("s_cvout1", CONV_OUT_T1)],
               inc=("s_dbg", 16))
        p.emit('gp', lambda e: e.dma_start(out=dbg["xnT"], in_=xnT),
               waits=[("s_xnTe", T_XNTE_ALL)], inc=("s_dbg", 16))
        p.emit('gp', lambda e: e.dma_start(out=dbg["Q"], in_=Qsb[:, :]),
               waits=[("s_qe", T_QE_ALL)], inc=("s_dbg", 16))
        p.emit('gp', lambda e: e.dma_start(out=dbg["K"], in_=Ksb[:, :]),
               inc=("s_dbg", 16))
        p.emit('gp', lambda e: e.dma_start(out=dbg["V"], in_=Vsb[:, :]),
               waits=[("s_ve", T_VE_ALL)], inc=("s_dbg", 16))
        p.emit('gp', lambda e: e.dma_start(out=dbg["hw"], in_=hw[:, :]),
               waits=[("s_hw", T_HW)], inc=("s_dbg", 16))
        p.emit('gp', lambda e: e.dma_start(out=dbg["ce"], in_=ce[:, :]),
               waits=[("s_cxe", T_CXE)], inc=("s_dbg", 16))
        p.emit('gp', lambda e: e.dma_start(out=dbg["lt"], in_=ltok[:, :]),
               waits=[("s_rte", T_RTE)], inc=("s_dbg", 16))

    T_SEND0 = p.tot("s_send0"); T_SEND1 = p.tot("s_send1")
    if not PHASE1_ONLY:
        p.emit('gp', lambda e: e.collective_compute(
            "AllGather", ALU.bypass, replica_groups=[list(range(NC))],
            ins=[ctx_send.opt()], outs=[gath.opt()]),
            waits=[("s_send0", T_SEND0), ("s_send1", T_SEND1)], inc=("s_cc", 1))


        def mk_gathread(e):
            pid = e.partition_id()
            g3 = gath.rearrange("(k p) (n t) -> n p k t", p=128, t=1024)
            return e.dma_start(out=ctxr, in_=g3[pid])
        p.emit('sp', mk_gathread, waits=[("s_cc", 1), ("s_gw0", T_G_ALL0),
                                     ("s_gw1", T_G_ALL1),
                                     ("s_dbg", p.tot("s_dbg"))], inc=("s_ctxr", 16))
        p.emit('sp', lambda e: e.dma_start(out=bor[0:1, 0:DM], in_=biasrows[0:1, :]),
               inc=("s_ctxr", 16))
        p.emit('sp', lambda e: e.dma_start(out=bor[0:1, DM:2 * DM], in_=biasrows[1:2, :]),
               inc=("s_ctxr", 16))
        T_CTXR = p.tot("s_ctxr")

        for tp_i in range(4):
            for half in range(2):
                for k in range(32):
                    slot = k % 4
                    wsem = "s_wst%d" % slot
                    p.emit('sp', (lambda k=k, half=half, slot=slot: lambda e: e.dma_start(
                        out=wostream[slot],
                        in_=wo_bf[k * 128:(k + 1) * 128, half * 2048:(half + 1) * 2048]))(),
                        waits=[("s_wmm", p.tot("s_wmm") - (0 if k < 4 else 3)),
                               ("s_cvout0", CONV_OUT_T0)],
                        inc=(wsem, 16))
                    t_w = p.tot(wsem)

                    def mk_wo(k=k, tp_i=tp_i, slot=slot):
                        def fn(e):
                            inst = None
                            for ti in range(2):
                                tau = tp_i * 2 + ti
                                lhsT = ctxr[:, k * 1024 + tau * 128: k * 1024 + (tau + 1) * 128]
                                for j in range(4):
                                    inst = e.matmul(bank(ti * 4 + j), lhsT,
                                                    wostream[slot][:, j * 512:(j + 1) * 512],
                                                    start=(k == 0), stop=False)
                            return inst
                        return fn
                    wl = [(wsem, t_w)]
                    if k == 0:
                        wl += [("s_ctxr", T_CTXR), ("s_ev1", p.tot("s_ev1")),
                               ("s_g", T_SILU)]
                    p.emit('pe', mk_wo(), waits=wl, inc=("s_wmm", 1))

                def mk_wobias(half=half):
                    def fn(e):
                        inst = None
                        for ti in range(2):
                            for j in range(4):
                                col = half * 2048 + j * 512
                                e.matmul(bank(ti * 4 + j), onesrow_b[:, :],
                                         bor[0:1, col:col + 512], start=False, stop=False)
                                inst = e.matmul(bank(ti * 4 + j), onesrow_b[:, :],
                                                bor[0:1, DM + col:DM + col + 512],
                                                start=False, stop=True)
                        return inst
                    return fn
                p.emit('pe', mk_wobias(), waits=[("s_ctxr", T_CTXR)], inc=("s_bmm", 1))
                t_wod = p.tot("s_bmm")
                for ti in range(2):
                    tau = tp_i * 2 + ti
                    p.emit('gp', (lambda tau=tau, half=half, ti=ti: lambda e: e.dma_start(
                        out=ghalf[ti], in_=g_dram[tau * 128:(tau + 1) * 128,
                                                  half * 2048:(half + 1) * 2048]))(),
                        waits=[("s_gw0", T_G_ALL0), ("s_gw1", T_G_ALL1),
                               ("s_ev1", p.tot("s_ev1")),
                               ("s_dbg", p.tot("s_dbg"))],
                        inc=("s_gh", 16))
                    p.emit('sp', (lambda tau=tau, half=half, ti=ti: lambda e: e.dma_start(
                        out=xohalf[ti], in_=xown[tau * 128:(tau + 1) * 128,
                                                 half * 2048:(half + 1) * 2048]))(),
                        waits=[("s_ev1", p.tot("s_ev1")),
                               ("s_dbg", p.tot("s_dbg"))],
                        inc=("s_xo", 16))
                    t_gh = p.tot("s_gh"); t_xo = p.tot("s_xo")

                    def mk_fin(ti=ti):
                        def fn(e):
                            _ch("dve", e, e.tensor_mul(ohalf[ti], bank(ti * 4, 4),
                                                       ghalf[ti]))
                            return e.tensor_add(ohalf[ti], ohalf[ti], xohalf[ti])
                        return fn
                    p.emit('dve', mk_fin(), waits=[("s_bmm", t_wod), ("s_gh", t_gh),
                                                   ("s_xo", t_xo),
                                                   ("s_out", p.tot("s_out"))],
                           inc=("s_ev1", 1))
                    p.emit('gp', (lambda tau=tau, half=half, ti=ti: lambda e: e.dma_start(
                        out=out_ext[tau * 128:(tau + 1) * 128,
                                    half * 2048:(half + 1) * 2048],
                        in_=ohalf[ti]))(),
                        waits=[("s_ev1", p.tot("s_ev1"))], inc=("s_out", 16))
        if DBG:
            p.emit('gp', lambda e: e.dma_start(out=dbg["scr2"][0:1, :], in_=scr2[0:1, :]),
                   waits=[("s_cc", 1)], inc=("s_dbg", 16))
            p.emit('gp', lambda e: e.dma_start(out=dbg["send"], in_=ctx_send),
                   waits=[("s_cc", 1)], inc=("s_dbg", 16))
            p.emit('gp', lambda e: e.dma_start(out=dbg["ctxr"], in_=ctxr[:, :]),
                   waits=[("s_ctxr", T_CTXR)], inc=("s_dbg", 16))
        T_OUT = p.tot("s_out")
        p.emit('gp', None, waits=[("s_out", T_OUT), ("s_dbg", p.tot("s_dbg"))])

    else:
        p.emit('gp', lambda e: e.dma_start(out=out_ext[0:128, :],
                                           in_=Qsb[:, 0:8192].bitcast(F32)),
               waits=[("s_qe", T_QE_ALL)], inc=("s_out", 16))
        p.emit('gp', None, waits=[("s_out", p.tot("s_out")),
                                  ("s_dbg", p.tot("s_dbg")),
                                  ("s_cvout0", CONV_OUT_T0),
                                  ("s_cvout1", CONV_OUT_T1),
                                  ("s_gw0", T_G_ALL0), ("s_gw1", T_G_ALL1)])

    semnames = sorted({nm for (_, ws, _, inc, _t) in p.ops
                       for nm in ([w[0] for w in ws] + ([inc[0]] if inc else []))})
    import contextlib
    stack = contextlib.ExitStack()
    sems = {}

    with nc.Block() as block:
        for sn in semnames:
            sems[sn] = stack.enter_context(nc.semaphore(sn))
        for key in ("dve", "act", "gp"):
            _CHSEMS[key] = stack.enter_context(nc.semaphore("s_ch_" + key))
        _CH.update({"dve": 0, "act": 0, "gp": 0})

        def runner(engine_key):
            def run(eng):
                for (ek, ws, fn, inc, _tag) in p.ops:
                    if ek != engine_key:
                        continue
                    for (sn, v) in ws:
                        eng.wait_ge(sems[sn], v)
                    if fn is None:
                        continue
                    inst = fn(eng)
                    if inc is not None:
                        assert inst is not None
                        inst.then_inc(sems[inc[0]], inc[1])
            return run

        block.tensor(runner('pe'))
        block.scalar(runner('act'))
        block.vector(runner('dve'))
        block.gpsimd(runner('gp'))
        block.sync(runner('sp'))

    if return_plan:
        return nc, p
    return nc


_NC_CACHE = {}


def _host_inputs(inputs):
    x = np.ascontiguousarray(np.asarray(inputs["L0_attn_hidden_states"],
                                        dtype=np.float32).reshape(S, DM))
    Wq = np.asarray(inputs["L0_attn_Wq"], np.float32)
    bq = np.asarray(inputs["L0_attn_bq"], np.float32)
    Wk = np.asarray(inputs["L0_attn_Wk"], np.float32)
    bk = np.asarray(inputs["L0_attn_bk"], np.float32)
    Wv = np.asarray(inputs["L0_attn_Wv"], np.float32)
    bv = np.asarray(inputs["L0_attn_bv"], np.float32)
    Wo = np.ascontiguousarray(np.asarray(inputs["L0_attn_Wo"], np.float32))
    bo_ = np.asarray(inputs["L0_attn_bo"], np.float32)
    Wb = np.asarray(inputs["L0_ffn_W_base"], np.float32)
    bb_ = np.asarray(inputs["L0_ffn_b_base"], np.float32)
    Wr = np.asarray(inputs["L0_ffn_W_router"], np.float32)
    A = np.asarray(inputs["L0_ffn_A"], np.float32)
    B = np.asarray(inputs["L0_ffn_B"], np.float32)

    j = np.arange(64, dtype=np.float32) / np.float32(64.0)
    inv = (np.float32(1.0) / np.power(np.float32(10000.0), j)).astype(np.float32)
    pos = np.arange(S, dtype=np.float32)
    ang = (pos[:, None] * inv[None, :]).astype(np.float32)
    cosT = np.cos(ang).T.astype(np.float32)
    sinT = np.sin(ang).T.astype(np.float32)
    csA = np.concatenate([cosT, sinT], axis=0)
    csB = np.concatenate([sinT, cosT], axis=0)
    cs_tab = np.ascontiguousarray(np.concatenate([csA, csB], axis=1).astype(np.float32))
    swapm_np = np.zeros((128, 128), np.float32)
    for i in range(128):
        swapm_np[i, (i + 64) % 128] = 1.0

    mask_np = np.zeros((128, 4 * 512), dtype=ml_dtypes.bfloat16)
    for c in range(4):
        kk = np.arange(128)[:, None] + 128 * c
        qq = np.arange(512)[None, :]
        mc = (kk <= qq).astype(ml_dtypes.bfloat16)
        mask_np[:, c * 512:(c + 1) * 512] = mc

    Ef_np = np.zeros((8, 128), np.float32)
    for e in range(8):
        Ef_np[e, e * 16:(e + 1) * 16] = 1.0
    cvec_np = np.zeros((128, 4), np.float32)
    cvec_np[:, 0] = SHIFT; cvec_np[:, 1] = GAMMA; cvec_np[:, 2] = EPS

    common = dict(
        x=x, wo=Wo,
        bo=np.ascontiguousarray(bo_.reshape(1, DM)),
        wbT=np.ascontiguousarray(Wb.T),
        bb=np.ascontiguousarray(bb_.reshape(1, DM)),
        wrT=np.ascontiguousarray(Wr.T),
        aT=np.ascontiguousarray(A.reshape(128, DM).T),
        bTm=np.ascontiguousarray(B.transpose(0, 2, 1).reshape(128, DM)),
        cs=cs_tab, masks=mask_np,
        identb=np.eye(128, dtype=ml_dtypes.bfloat16),
        identf=np.eye(128, dtype=np.float32),
        ident8f=np.eye(8, dtype=np.float32),
        onescol=np.ones((128, 1), ml_dtypes.bfloat16),
        onesrow_b=np.ones((1, 128), ml_dtypes.bfloat16),
        onesrow_f=np.ones((1, 128), np.float32),
        Ef=Ef_np, cvec=cvec_np, swapm=swapm_np,
    )
    in_maps = []
    for g in range(NC):
        m = dict(common)
        m.update(
            xown=np.ascontiguousarray(x[g * OWN:(g + 1) * OWN, :]),
            wq=np.ascontiguousarray(Wq[:, g * 512:(g + 1) * 512]),
            wk=np.ascontiguousarray(Wk[:, g * 128:(g + 1) * 128]),
            wv=np.ascontiguousarray(Wv[:, g * 128:(g + 1) * 128]),
            bq4=np.ascontiguousarray(bq[g * 512:(g + 1) * 512].reshape(4, 128).T),
            bk1=np.ascontiguousarray(bk[g * 128:(g + 1) * 128].reshape(1, 128).T),
            bv1=np.ascontiguousarray(bv[g * 128:(g + 1) * 128].reshape(1, 128).T),
        )
        in_maps.append(m)
    return in_maps


def kernel(**inputs):
    in_maps = _host_inputs(inputs)
    if "nc" not in _NC_CACHE:
        _NC_CACHE["nc"] = build_program()
    nc = _NC_CACHE["nc"]
    res = run_bass_kernel_spmd(nc, in_maps, core_ids=list(range(NC)))
    out = np.concatenate([res.results[g]["out"] for g in range(NC)], axis=0)
    return np.ascontiguousarray(out.reshape(1, S, DM))


# version 14
# speedup vs baseline: 1.0068x; 1.0068x over previous
import math
import os
import numpy as np
import ml_dtypes

import concourse.bass as bass
import concourse.mybir as mybir
from concourse.bass_utils import run_bass_kernel_spmd

F32 = mybir.dt.float32
BF16 = mybir.dt.bfloat16
AF = mybir.ActivationFunctionType
ALU = mybir.AluOpType
AX = mybir.AxisListType

NC = 8
S = 8192
DM = 4096
HQ = 4
OWN = 1024
TPB = 8
NQB = 16
GAMMA = 1.0 / math.sqrt(128.0)
SHIFT = -20.0
EPS = 1e-5




_CH = {"dve": 0, "act": 0, "gp": 0}
_CHSEMS = {}


def _ch(key, e, inst):
    _CH[key] += 1
    e_sem = _CHSEMS[key]
    inst.then_inc(e_sem, 1)
    e.wait_ge(e_sem, _CH[key])
    return inst


def _mk_rope(e_qtmp, e_swp, e_cs, e_tmp, dsts):
    def fn(e):
        inst = None
        for j in range(2):
            c0, c1 = j * 512, (j + 1) * 512
            tA = e_tmp[0:64, 0:512]; tB = e_tmp[0:64, 512:1024]
            tC = e_tmp[64:128, 0:512]; tD = e_tmp[64:128, 512:1024]
            e.tensor_mul(tA, e_qtmp[0:64, c0:c1], e_cs[0:64, c0:c1])
            _ch("dve", e, e.tensor_mul(tB, e_swp[0:64, c0:c1],
                                       e_cs[0:64, 1024 + c0:1024 + c1]))
            e.tensor_sub(dsts[0][:, c0:c1], tA, tB)
            e.tensor_mul(tC, e_swp[64:128, c0:c1], e_cs[64:128, c0:c1])
            _ch("dve", e, e.tensor_mul(tD, e_qtmp[64:128, c0:c1],
                                       e_cs[64:128, 1024 + c0:1024 + c1]))
            inst = e.tensor_add(dsts[1][:, c0:c1], tC, tD)
        return inst
    return fn

class Plan:

    def __init__(self):
        self.ops = []
        self.cnt = {}

    label = ""

    def emit(self, engine, fn, waits=(), inc=None):
        ws = [(s, v) for (s, v) in waits if v > 0]
        if inc is not None:
            self.cnt[inc[0]] = self.cnt.get(inc[0], 0) + inc[1]
        self.ops.append((engine, ws, fn, inc, Plan.label))

    def tot(self, sem):
        return self.cnt.get(sem, 0)


def build_program(return_plan=False):
    nc = bass.Bass("TRN2", target_bir_lowering=False,
                   detect_race_conditions=not os.environ.get("KMK_NORACE"))

    def ext(name, shape, dtype=F32):
        return nc.dram_tensor(name, shape, dtype, kind="ExternalInput").ap()

    x = ext("x", [S, DM])
    xown = ext("xown", [OWN, DM])
    wq = ext("wq", [DM, 512]); wk = ext("wk", [DM, 128]); wv = ext("wv", [DM, 128])
    bq4 = ext("bq4", [128, 4]); bk1 = ext("bk1", [128, 1]); bv1 = ext("bv1", [128, 1])
    wo = ext("wo", [DM, DM]); bo = ext("bo", [1, DM])
    wbT = ext("wbT", [DM, DM]); bb = ext("bb", [1, DM])
    wrT = ext("wrT", [DM, 8])
    aT = ext("aT", [DM, 128])
    bTm = ext("bTm", [128, DM])
    cs = ext("cs", [128, 2 * S])
    swapm_d = ext("swapm", [128, 128], F32)
    masks_d = ext("masks", [128, 4 * 512], BF16)
    identb_d = ext("identb", [128, 128], BF16)
    ident8f_d = ext("ident8f", [8, 8], F32)
    identf_d = ext("identf", [128, 128], F32)
    onescol_d = ext("onescol", [128, 1], BF16)
    onesrow_b_d = ext("onesrow_b", [1, 128], BF16)
    onesrow_f_d = ext("onesrow_f", [1, 128], F32)
    Ef_d = ext("Ef", [8, 128], F32)
    cvec_d = ext("cvec", [128, 4], F32)

    out_ext = nc.dram_tensor("out", [OWN, DM], F32, kind="ExternalOutput").ap()

    w_bf = nc.dram_tensor("w_bf", [DM, 768], BF16).ap()
    wo_bf = nc.dram_tensor("wo_bf", [DM, DM], BF16).ap()
    wb_bf = nc.dram_tensor("wb_bf", [DM, DM], BF16).ap()
    a_bf = nc.dram_tensor("a_bf", [128, DM], BF16).ap()
    r_bf = nc.dram_tensor("r_bf", [128, 256], BF16).ap()
    bT_bf = nc.dram_tensor("bT_bf", [128, DM], BF16).ap()
    ctx_send = nc.dram_tensor("ctx_send", [512, S], BF16).ap()
    gath = nc.dram_tensor("gath", [NC * 512, S], BF16, addr_space="Shared").ap()
    DBG = bool(os.environ.get("KMK_DEBUG"))
    dbgkind = "ExternalOutput" if DBG else "Internal"
    g_dram = nc.dram_tensor("g_dram", [OWN, DM], F32, kind=dbgkind).ap()
    biasrows = nc.dram_tensor("biasrows", [4, DM], BF16).ap()
    dbg = {}
    if DBG:
        dbg["xnT"] = nc.dram_tensor("dbg_xnT", [128, TPB * DM], BF16, kind="ExternalOutput").ap()
        dbg["Q"] = nc.dram_tensor("dbg_Q", [128, HQ * 2048], BF16, kind="ExternalOutput").ap()
        dbg["K"] = nc.dram_tensor("dbg_K", [128, S], BF16, kind="ExternalOutput").ap()
        dbg["V"] = nc.dram_tensor("dbg_V", [128, S], BF16, kind="ExternalOutput").ap()
        dbg["hw"] = nc.dram_tensor("dbg_hw", [128, 1024], BF16, kind="ExternalOutput").ap()
        dbg["ce"] = nc.dram_tensor("dbg_ce", [128, 1024], F32, kind="ExternalOutput").ap()
        dbg["lt"] = nc.dram_tensor("dbg_lt", [128, 64], F32, kind="ExternalOutput").ap()
        dbg["scr2"] = nc.dram_tensor("dbg_scr2", [128, 1024], F32, kind="ExternalOutput").ap()
        dbg["ctxr"] = nc.dram_tensor("dbg_ctxr", [128, 32 * 1024], BF16, kind="ExternalOutput").ap()
        dbg["send"] = nc.dram_tensor("dbg_send", [512, S], BF16, kind="ExternalOutput").ap()
        dbg["wbf"] = nc.dram_tensor("dbg_wbf", [DM, 768], BF16, kind="ExternalOutput").ap()
        dbg["qtmp"] = nc.dram_tensor("dbg_qtmp", [128, 1024], F32, kind="ExternalOutput").ap()
        dbg["ps03"] = nc.dram_tensor("dbg_ps03", [128, 2048], F32, kind="ExternalOutput").ap()
        dbg["kps"] = nc.dram_tensor("dbg_kps", [128, 1024], F32, kind="ExternalOutput").ap()
        dbg["abf"] = nc.dram_tensor("dbg_abf", [128, DM], BF16, kind="ExternalOutput").ap()

    p = Plan()

    def sbuf(name, shape, dtype):
        return nc.sbuf_tensor(name, shape, dtype).__enter__()

    Qsb = sbuf("Qsb", [128, HQ * 2048], BF16)
    Ksb = sbuf("Ksb", [128, S], BF16)
    Vsb = sbuf("Vsb", [128, S], BF16)
    identb = sbuf("identb_s", [128, 128], BF16)
    identf = sbuf("identf_s", [128, 128], F32)
    ident8f = sbuf("ident8f_s", [8, 8], F32)
    onescol = sbuf("onescol_s", [128, 1], BF16)
    onesrow_b = sbuf("onesrow_b_s", [1, 128], BF16)
    onesrow_f = sbuf("onesrow_f_s", [1, 128], F32)
    Ef = sbuf("Ef_s", [8, 128], F32)
    cvec = sbuf("cvec_s", [128, 4], F32)
    swapm = sbuf("swapm_s", [128, 128], F32)
    ctxn = sbuf("ctxn", [128, 1024], BF16)

    ARENA_B = 158912
    arena = sbuf("arena", [128, ARENA_B], mybir.dt.uint8)

    def carve(off_b, cols, dtype):
        esz = mybir.dt.size(dtype)
        return arena[:, off_b:off_b + cols * esz].bitcast(dtype)

    o = 0
    xb = [carve(o, DM, F32), carve(o + DM * 4, DM, F32)]; o += 2 * DM * 4
    xn2 = [carve(o, DM, BF16), carve(o + DM * 2, DM, BF16)]; o += DM * 4
    cs_sb = carve(o, 2048, F32); o += 8192
    qtmp = carve(o, 1024, F32); o += 4096
    ropet = carve(o, 1024, F32); o += 4096
    stt = carve(o, 32, F32); o += 128
    wsb = [carve(o + i * 1536, 768, BF16) for i in range(3)]; o += 3 * 1536
    cvst = carve(o, 1024, F32); o += 4096
    cvst_b = carve(o, 1024, BF16); o += 2048
    bqs = carve(o, 4, F32); o += 16
    bks = carve(o, 1, F32); o += 4
    bvs = carve(o, 1, F32); o += 4
    P1_LN_END = o
    XNT_OFF = 76960
    assert o <= XNT_OFF, o
    xnT = carve(XNT_OFF, TPB * DM, BF16)
    xnTv = xnT.rearrange("p (tt kk c) -> p tt kk c", tt=8, kk=32, c=128)

    def mv_xnT(k, j):
        return xnTv[:, 4 * j:4 * j + 4, k, :]

    o = 0
    wbstream = [carve(o + i * 4096, 2048, BF16) for i in range(3)]; o += 3 * 4096
    aT_sb = carve(o, DM, BF16); o += DM * 2
    bT_sb = carve(o, DM, BF16); o += DM * 2
    wrT_sb = carve(o, 256, BF16); o += 512
    ltT = carve(o, 1024, F32); o += 4096
    ctT = carve(o, 1024, F32); o += 4096
    ce = carve(o, 1024, F32); o += 4096
    ltok = carve(o - 4096, 64, F32)
    cbuf = carve(o - 4096 + 256, 64, F32)
    tp2 = carve(o - 4096 + 512, 64, F32)
    hw = carve(o, 1024, BF16); o += 2048
    bbr = carve(o, 2 * DM, BF16); o += DM * 4
    gstage = [carve(o, 2048, F32), carve(o + 8192, 2048, F32)]; o += 16384
    assert o <= XNT_OFF, o

    o = XNT_OFF + TPB * DM * 2
    PTs = carve(o, 2048, BF16); o += 4096
    mask_sb = carve(o, 4 * 512, BF16); o += 4096
    scr2 = carve(o, 1024, F32); o += 4096
    rb = carve(o, 512, F32); o += 2048
    assert o <= ARENA_B, o

    o = 0
    ctxr = carve(o, 32 * 1024, BF16); o += 32 * 1024 * 2
    wostream = [carve(o + i * 4096, 2048, BF16) for i in range(4)]; o += 4 * 4096
    ghalf = [carve(o, 2048, F32), carve(o + 8192, 2048, F32)]; o += 16384
    xohalf = [carve(o, 2048, F32), carve(o + 8192, 2048, F32)]; o += 16384
    ohalf = [carve(o, 2048, F32), carve(o + 8192, 2048, F32)]; o += 16384
    bor = carve(o, 2 * DM, BF16); o += DM * 4
    assert o <= ARENA_B

    PS = nc.psum_tensor("PS", [128, 4096], F32).__enter__()

    def bank(b, w=1):
        return PS[:, b * 512:(b + w) * 512]

    def bankb16(b):
        return PS[:, b * 512:(b + 1) * 512].bitcast(BF16)

    SHIFT_AP = cvec[:, 0:1]; GAMMA_AP = cvec[:, 1:2]; EPS_AP = cvec[:, 2:3]

    pre_list = [
        (cvec[:, :], cvec_d), (identb[:, :], identb_d), (identf[:, :], identf_d),
        (ident8f[:, :], ident8f_d), (onescol[:, :], onescol_d),
        (onesrow_b[:, :], onesrow_b_d), (onesrow_f[:, :], onesrow_f_d),
        (Ef[:, :], Ef_d), (swapm[:, :], swapm_d),
        (bqs[:, :], bq4), (bks[:, :], bk1), (bvs[:, :], bv1),
    ]
    for dst, src in pre_list:
        p.emit('sp', (lambda dst=dst, src=src: lambda e: e.dma_start(out=dst, in_=src))(),
               inc=("s_pre", 16))
    PRE_T = p.tot("s_pre")

    for (srcrow, r0) in ((bo, 0), (bb, 2)):
        p.emit('sp', (lambda srcrow=srcrow: lambda e: e.dma_start(
            out=xb[0][0:1, :], in_=srcrow))(),
            waits=[("s_brow", p.tot("s_brow"))], inc=("s_browh", 16))
        p.emit('gp', lambda e: e.tensor_copy(xn2[0][0:1, :], xb[0][0:1, :]),
               waits=[("s_browh", p.tot("s_browh")),
                      ("s_browd", p.tot("s_browd"))], inc=("s_brow", 1))
        p.emit('gp', (lambda r0=r0: lambda e: e.dma_start(
            out=biasrows[r0:r0 + 1, :], in_=xn2[0][0:1, :]))(),
            waits=[("s_brow", p.tot("s_brow"))], inc=("s_browd", 16))
        p.emit('dve', lambda e: e.tensor_sub(xb[1][0:1, :], xb[0][0:1, :], xn2[0][0:1, :]),
               waits=[("s_brow", p.tot("s_brow")), ("s_browd", p.tot("s_browd"))],
               inc=("s_brow", 1))
        p.emit('gp', lambda e: e.tensor_copy(xn2[0][0:1, :], xb[1][0:1, :]),
               waits=[("s_brow", p.tot("s_brow")), ("s_browd", p.tot("s_browd"))],
               inc=("s_brow", 1))
        p.emit('gp', (lambda r0=r0: lambda e: e.dma_start(
            out=biasrows[r0 + 1:r0 + 2, :], in_=xn2[0][0:1, :]))(),
            waits=[("s_brow", p.tot("s_brow"))], inc=("s_browd", 16))
    T_BROW = p.tot("s_brow")
    T_BROWD = p.tot("s_browd")

    conv_jobs = [("wqkv", k) for k in range(32)]
    conv_jobs += [("aT", c) for c in range(4)] + [("rT", 0)] + [("bT", c) for c in range(4)]
    conv_jobs += [("wb", (k, c)) for k in range(32) for c in range(4)]
    conv_jobs += [("wo", (k, c)) for k in range(32) for c in range(4)]
    N_WQKV_JOBS = 32

    cstate = {"idx": 0, "wqkv_out_t": None, "ffn_out_t": None}

    def conv_aps(job):
        kind, arg = job
        if kind == "wqkv":
            k = arg
            return [(wq[k * 128:(k + 1) * 128, :], 0, 512),
                    (wk[k * 128:(k + 1) * 128, :], 512, 128),
                    (wv[k * 128:(k + 1) * 128, :], 640, 128)], \
                w_bf[k * 128:(k + 1) * 128, :], 768
        if kind == "wo":
            k, c = arg
            return [(wo[k * 128:(k + 1) * 128, c * 1024:(c + 1) * 1024], 0, 1024)], \
                wo_bf[k * 128:(k + 1) * 128, c * 1024:(c + 1) * 1024], 1024
        if kind == "wb":
            k, c = arg
            return [(wbT[k * 128:(k + 1) * 128, c * 1024:(c + 1) * 1024], 0, 1024)], \
                wb_bf[k * 128:(k + 1) * 128, c * 1024:(c + 1) * 1024], 1024
        if kind == "aT":
            c = arg
            srcv = aT.rearrange("(a p) c -> p a c", p=128)
            return [(srcv[:, c * 8:(c + 1) * 8, :], 0, 1024)], \
                a_bf[:, c * 1024:(c + 1) * 1024], 1024
        if kind == "rT":
            srcv = wrT.rearrange("(a p) c -> p a c", p=128)
            return [(srcv, 0, 256)], r_bf[:, :], 256
        if kind == "bT":
            c = arg
            return [(bTm[:, c * 1024:(c + 1) * 1024], 0, 1024)], \
                bT_bf[:, c * 1024:(c + 1) * 1024], 1024

    def emit_conv_job():
        if cstate["idx"] >= len(conv_jobs):
            return
        job = conv_jobs[cstate["idx"]]
        cu = cstate["idx"]
        cstate["idx"] += 1
        segs, dst, width = conv_aps(job)
        for (sap, off, w) in segs:
            p.emit('sp', (lambda sap=sap, off=off, w=w:
                          lambda e: e.dma_start(out=cvst[:, off:off + w], in_=sap))(),
                   waits=[("s_cv", cu)],
                   inc=("s_cvin0", 16))
        t_in = p.tot("s_cvin0")
        ceng = 'gp' if cu % 2 == 0 else 'dve'
        p.emit(ceng, (lambda width=width:
                      lambda e: e.tensor_copy(cvst_b[:, 0:width],
                                              cvst[:, 0:width]))(),
               waits=[("s_cvin0", t_in), ("s_cvout0", p.tot("s_cvout0"))],
               inc=("s_cv", 1))
        p.emit('sp', (lambda dst=dst, width=width:
                      lambda e: e.dma_start(out=dst, in_=cvst_b[:, 0:width]))(),
               waits=[("s_cv", p.tot("s_cv"))], inc=("s_cvout0", 16))
        if cstate["idx"] == N_WQKV_JOBS:
            cstate["wqkv_out_t"] = (p.tot("s_cvout0"), 0)
        if cstate["idx"] == 169:
            cstate["ffn_out_t"] = p.tot("s_cvout0")

    def trickle(n):
        for _ in range(n):
            emit_conv_job()

    if os.environ.get("KMK_CONVFIRST"):
        while cstate["idx"] < len(conv_jobs):
            emit_conv_job()

    NT = 72
    woff = {0: 0, 1: 128, 2: 256, 3: 384, 4: 512, 5: 640}
    PHASE1_ONLY = os.environ.get("KMK_PHASE") == "1"
    att_stmm = {}

    p.emit('sp', lambda e: e.dma_start(out=mask_sb, in_=masks_d),
           inc=("s_msk0", 16))

    def emit_tile(t):
        b = t // TPB
        own = (b == 8)
        src = (xown[(t - 64) * 128:(t - 63) * 128, :] if own
               else x[t * 128:(t + 1) * 128, :])
        xslot = xb[t % 2]
        xsem = "s_x%d" % (t % 2)
        Plan.label = f"tile{t}"
        p.emit('sp', (lambda src=src, xslot=xslot: lambda e: e.dma_start(out=xslot, in_=src))(),
               waits=[("s_xn", max(0, t - 1)), ("s_brow", T_BROW if t < 2 else 0),
                      ("s_browd", T_BROWD if t < 2 else 0)],
               inc=(xsem, 16))
        t_x = p.tot(xsem)
        if not own and t % TPB == 0:
            p.emit('sp', (lambda b=b: lambda e: e.dma_start(
                out=cs_sb[:, 0:1024], in_=cs[:, b * 1024:(b + 1) * 1024]))(),
                waits=[("s_qe", p.tot("s_qe"))], inc=("s_cs", 16))
            p.emit('sp', (lambda b=b: lambda e: e.dma_start(
                out=cs_sb[:, 1024:2048], in_=cs[:, S + b * 1024:S + (b + 1) * 1024]))(),
                inc=("s_cs", 16))
        trickle(4)

        p.emit('dve', (lambda xslot=xslot, sb_=(t % 2) * 16: lambda e: e.reduce_sum(
            stt[:, sb_:sb_ + 1], xslot, axis=AX.X))(),
            waits=[(xsem, t_x), ("s_xn", max(0, t - 1)), ("s_stats", max(0, t - 1)),
                   ("s_pre", PRE_T if t < 2 else 0)],
            inc=("s_dsum", 1))
        xn_sl = xn2[t % 2]
        sb_ = (t % 2) * 16
        p.emit('act', (lambda xslot=xslot, xn_sl=xn_sl, sb_=sb_: lambda e: e.activation(
            xn_sl[:, :], xslot, AF.Square, accum_out=stt[:, sb_ + 1:sb_ + 2]))(),
            waits=[(xsem, t_x), ("s_tp", max(0, 4 * (t - 1))),
                   ("s_browd", T_BROWD if t < 2 else 0),
                   ("s_vtp", (t - 1) // 8 if t % TPB == 1 else 0),
                   ("s_pre", PRE_T if t < 2 else 0)], inc=("s_sq", 1))

        def stats1(e, b_=sb_):
            _ch("dve", e, e.tensor_scalar_mul(stt[:, b_ + 2:b_ + 3], stt[:, b_:b_ + 1],
                                              1.0 / 4096.0))
            e.tensor_scalar_mul(stt[:, b_ + 3:b_ + 4], stt[:, b_ + 1:b_ + 2], 1.0 / 4096.0)
            _ch("dve", e, e.tensor_mul(stt[:, b_:b_ + 1], stt[:, b_ + 2:b_ + 3],
                                       stt[:, b_ + 2:b_ + 3]))
            return e.tensor_sub(stt[:, b_ + 3:b_ + 4], stt[:, b_ + 3:b_ + 4],
                                stt[:, b_:b_ + 1])
        p.emit('dve', stats1, waits=[("s_sq", t + 1), ("s_dsum", t + 1)], inc=("s_v1", 1))
        p.emit('act', (lambda b_=sb_: lambda e: e.activation(
            stt[:, b_ + 4:b_ + 5], stt[:, b_ + 3:b_ + 4], AF.Sqrt, bias=EPS_AP))(),
               waits=[("s_v1", t + 1)], inc=("s_v2", 1))

        def stats2(e, b_=sb_):
            _ch("dve", e, e.reciprocal(stt[:, b_ + 5:b_ + 6], stt[:, b_ + 4:b_ + 5]))
            return e.tensor_scalar(stt[:, b_ + 6:b_ + 7], stt[:, b_ + 2:b_ + 3],
                                   stt[:, b_ + 5:b_ + 6], -1.0, ALU.mult, ALU.mult)
        p.emit('dve', stats2, waits=[("s_v2", t + 1)], inc=("s_stats", 1))
        p.emit('act', (lambda xslot=xslot, xn_sl=xn_sl, b_=sb_: lambda e: e.activation(
            xn_sl[:, :], xslot, AF.Identity, bias=stt[:, b_ + 6:b_ + 7],
            scale=stt[:, b_ + 5:b_ + 6]))(),
            waits=[("s_stats", t + 1)], inc=("s_xn", 1))

        for beta in range(4):
            tb = 2 + beta

            def mk_tp(beta=beta, tb=tb, xn_sl=xn_sl):
                def fn(e):
                    inst = None
                    for j in range(8):
                        k = beta * 8 + j
                        inst = e.matmul(bankb16(tb)[:, j * 128:(j + 1) * 128],
                                        xn_sl[:, k * 128:(k + 1) * 128], identb[:, :],
                                        is_transpose=True,
                                        start=(j == 0), stop=(j == 7))
                    return inst
                return fn
            p.emit('pe', mk_tp(),
                   waits=[("s_xn", t + 1), ("s_pre", PRE_T),
                          ("s_xnTe", max(0, 4 * t + beta - 3)),
                          ("s_exp", p.tot("s_exp")),
                          ("s_qe", p.tot("s_qe")), ("s_vs", p.tot("s_vs"))],
                   inc=("s_tp", 1))
            tau = t % TPB
            dst = xnT[:, tau * 4096 + beta * 1024: tau * 4096 + (beta + 1) * 1024]
            p.emit('dve', (lambda dst=dst, tb=tb: lambda e: e.tensor_copy(dst, bankb16(tb)))(),
                   waits=[("s_tp", 4 * t + beta + 1)], inc=("s_xnTe", 1))

    def emit_qkv(b):
        Plan.label = f"qkv_b{b}"
        t_xnTe = p.tot("s_xnTe")
        t_cs = p.tot("s_cs")
        war_stmm = att_stmm.get(2 * b - 3, 0)
        for oset in range(2):
            ocs = [0, 1, 2] if oset == 0 else [3, 4, 5]
            q0 = p.tot("s_qe"); v0 = p.tot("s_ve")
            for k in range(32):
                slot = k % 3
                wsem = "s_wst%d" % slot
                p.emit('sp', (lambda k=k, slot=slot: lambda e: e.dma_start(
                    out=wsb[slot], in_=w_bf[k * 128:(k + 1) * 128, :]))(),
                    waits=[("s_cvout0", cstate["wqkv_out_t"][0] if cstate["wqkv_out_t"] else 0),
                           ("s_wsb", p.tot("s_wsb") - (0 if k < 3 else 2))],
                    inc=(wsem, 16))
                t_wst = p.tot(wsem)

                def mk_qkv(k=k, slot=slot, ocs=ocs):
                    def fn(e):
                        inst = None
                        for si, oc in enumerate(ocs):
                            lhsT = wsb[slot][:, woff[oc]:woff[oc] + 128]
                            for j in range(2):
                                inst = e.matmul(bank(si * 2 + j), lhsT, mv_xnT(k, j),
                                                start=(k == 0), stop=(k == 31))
                        return inst
                    return fn
                wl = [(wsem, t_wst)]
                if k == 0:
                    wl += [("s_xnTe", t_xnTe), ("s_qe", q0), ("s_ve", v0),
                           ("s_cs", t_cs), ("s_cn", p.tot("s_cn")),
                           ("s_exp", p.tot("s_exp"))]
                p.emit('pe', mk_qkv(), waits=wl, inc=("s_wsb", 1))
            t_done = p.tot("s_wsb")

            for si, oc in enumerate(ocs):
                Plan.label = f"evac_b{b}_oc{oc}"
                pw = bank(si * 2, 2)
                if oc <= 3:
                    h = oc
                    qbase = (b % 2) * 4096 + h * 1024
                    dsts = [Qsb[0:64, qbase:qbase + 1024],
                            Qsb[64:128, qbase:qbase + 1024]]
                    p.emit('dve', (lambda pw=pw, h=h: lambda e: e.tensor_scalar_add(
                        qtmp[:, :], pw, bqs[:, h:h + 1]))(),
                        waits=[("s_wsb", t_done), ("s_cs", t_cs),
                               ("s_qe", p.tot("s_qe"))], inc=("s_qb", 1))
                    p.emit('pe', (lambda si=si: lambda e: [
                        e.matmul(bank(si * 2), swapm[:, :], qtmp[:, 0:512],
                                 start=True, stop=True),
                        e.matmul(bank(si * 2 + 1), swapm[:, :], qtmp[:, 512:1024],
                                 start=True, stop=True)][-1])(),
                        waits=[("s_qb", p.tot("s_qb"))], inc=("s_qsw", 1))
                    p.emit('dve', (lambda pw=pw, dsts=dsts: _mk_rope(
                        e_qtmp=qtmp, e_swp=pw, e_cs=cs_sb, e_tmp=ropet,
                        dsts=dsts))(),
                        waits=[("s_qsw", p.tot("s_qsw")),
                               ("s_stmm", war_stmm)], inc=("s_qe", 1))
                elif oc == 4:
                    dsts = [Ksb[0:64, b * 1024:(b + 1) * 1024],
                            Ksb[64:128, b * 1024:(b + 1) * 1024]]

                    p.emit('dve', (lambda pw=pw: lambda e: e.tensor_scalar_add(
                        qtmp[:, :], pw, bks[:, 0:1]))(),
                        waits=[("s_wsb", t_done), ("s_cs", t_cs),
                               ("s_qe", p.tot("s_qe"))], inc=("s_qb", 1))

                    p.emit('pe', (lambda si=si: lambda e: [
                        e.matmul(bank(si * 2), swapm[:, :], qtmp[:, 0:512],
                                 start=True, stop=True),
                        e.matmul(bank(si * 2 + 1), swapm[:, :], qtmp[:, 512:1024],
                                 start=True, stop=True)][-1])(),
                        waits=[("s_qb", p.tot("s_qb"))], inc=("s_qsw", 1))
                    p.emit('dve', (lambda pw=pw, dsts=dsts: _mk_rope(
                        e_qtmp=qtmp, e_swp=pw, e_cs=cs_sb, e_tmp=ropet,
                        dsts=dsts))(),
                        waits=[("s_qsw", p.tot("s_qsw"))], inc=("s_qe", 1))
                else:
                    vsl = xn2[1][:, 0:1024]
                    p.emit('act', (lambda pw=pw, vsl=vsl: lambda e: e.activation(
                        vsl, pw, AF.Identity, bias=bvs[:, 0:1]))(),
                        waits=[("s_wsb", t_done), ("s_vtp", p.tot("s_vtp"))],
                        inc=("s_vs", 1))

                    def mk_vtp(vsl=vsl):
                        def fn(e):
                            inst = None
                            for j in range(8):
                                inst = e.matmul(
                                    bankb16(6)[:, j * 128:(j + 1) * 128],
                                    vsl[:, j * 128:(j + 1) * 128], identb[:, :],
                                    is_transpose=True, start=(j == 0), stop=(j == 7))
                            return inst
                        return fn
                    p.emit('pe', mk_vtp(), waits=[("s_vs", p.tot("s_vs")),
                                                  ("s_qe", p.tot("s_qe")),
                                                  ("s_rec", p.tot("s_rec")),
                                                  ("s_xnTe", p.tot("s_xnTe"))],
                           inc=("s_vtp", 1))
                    p.emit('dve', (lambda b=b: lambda e: e.tensor_copy(
                        Vsb[:, b * 1024:(b + 1) * 1024], bankb16(6)))(),
                        waits=[("s_vtp", p.tot("s_vtp"))], inc=("s_ve", 1))

    def emit_attn(qb):
        if PHASE1_ONLY:
            att_stmm[qb] = p.tot("s_stmm")
            return
        b = qb // 2
        Plan.label = f"attn{qb}"
        kcmax = 4 * (qb + 1)
        qoff = (b % 2) * 4096 + (qb % 2) * 512
        t_xnTe0 = p.tot("s_xnTe"); t_ve0 = p.tot("s_ve")
        for hp in range(2):
            heads = (hp * 2, hp * 2 + 1)
            base_cmm = p.tot("s_cmm")
            base_stmm = p.tot("s_stmm")
            base_exp = p.tot("s_exp")
            for it in range(kcmax + 1):
                if it < kcmax:
                    kc = it
                    slot = kc % 2
                    sb0 = 2 + slot * 2

                    def mk_st(kc=kc, sb0=sb0, heads=heads, qoff=qoff):
                        def fn(e):
                            inst = None
                            for i, h in enumerate(heads):
                                inst = e.matmul(
                                    bank(sb0 + i),
                                    Ksb[:, kc * 128:(kc + 1) * 128],
                                    Qsb[:, qoff + h * 1024: qoff + h * 1024 + 512],
                                    start=True, stop=True)
                            return inst
                        return fn
                    expw = base_exp + (kc - 1) if kc >= 2 else base_exp
                    wl = [("s_exp", max(0, expw))]
                    if kc == 0 and hp == 0:
                        wl += [("s_qe", p.tot("s_qe")), ("s_ve", p.tot("s_ve")),
                               ("s_vs", p.tot("s_vs")),
                               ("s_xnTe", p.tot("s_xnTe"))]
                        if qb == 0:
                            wl.append(("s_msk0", 16))
                    p.emit('pe', mk_st(), waits=wl, inc=("s_stmm", 1))
                if it >= 1:
                    kcc = it - 1
                    slot = kcc % 2
                    sb0 = 2 + slot * 2
                    diag = kcc >= kcmax - 4
                    expwl = [("s_stmm", base_stmm + kcc + 1)]
                    if kcc >= 2:
                        expwl += [("s_cmm", base_cmm + kcc - 1)]
                    p.emit('act', (lambda sb0=sb0, slot=slot:
                                   lambda e: e.activation(
                                       PTs[:, slot * 1024:(slot + 1) * 1024],
                                       bank(sb0, 2), AF.Exp,
                                       bias=SHIFT_AP, scale=GAMMA_AP))(),
                           waits=expwl, inc=("s_exp", 1))
                    t_exp = p.tot("s_exp")
                    if diag:
                        c = kcc - (kcmax - 4)

                        def mk_msk(slot=slot, c=c):
                            def fn(e):
                                e.tensor_mul(
                                    PTs[:, slot * 1024:slot * 1024 + 512],
                                    PTs[:, slot * 1024:slot * 1024 + 512],
                                    mask_sb[:, c * 512:(c + 1) * 512])
                                return e.tensor_mul(
                                    PTs[:, slot * 1024 + 512:slot * 1024 + 1024],
                                    PTs[:, slot * 1024 + 512:slot * 1024 + 1024],
                                    mask_sb[:, c * 512:(c + 1) * 512])
                            return fn
                        p.emit('dve', mk_msk(), waits=[("s_exp", t_exp)],
                               inc=("s_msk", 1))
                    t_msk = p.tot("s_msk")

                    def mk_ctx(kcc=kcc, slot=slot, kcmax=kcmax):
                        def fn(e):
                            st = (kcc == 0); ed = (kcc == kcmax - 1)
                            for i in range(2):
                                e.matmul(
                                    bank(i), Vsb[:, kcc * 128:(kcc + 1) * 128],
                                    PTs[:, slot * 1024 + i * 512:
                                        slot * 1024 + (i + 1) * 512],
                                    start=st, stop=ed)
                            e.matmul(PS[0:1, 6 * 512:7 * 512], onescol[:, :],
                                     PTs[:, slot * 1024:slot * 1024 + 512],
                                     start=st, stop=ed)
                            return e.matmul(PS[0:1, 7 * 512:8 * 512], onescol[:, :],
                                            PTs[:, slot * 1024 + 512:slot * 1024 + 1024],
                                            start=st, stop=ed)
                        return fn
                    wl = [("s_exp", t_exp)]
                    if diag:
                        wl.append(("s_msk", t_msk))
                    if kcc == 0:
                        wl += [("s_cn", p.tot("s_cn")),
                               ("s_xnTe", t_xnTe0), ("s_ve", t_ve0)]
                    p.emit('pe', mk_ctx(), waits=wl, inc=("s_cmm", 1))
            t_cmm = p.tot("s_cmm")

            def mk_dev(e):
                e.reciprocal(scr2[0:1, 0:512], PS[0:1, 6 * 512:7 * 512])
                return e.reciprocal(scr2[0:1, 512:1024], PS[0:1, 7 * 512:8 * 512])
            p.emit('dve', mk_dev, waits=[("s_cmm", t_cmm)],
                   inc=("s_rec", 1))
            t_rec = p.tot("s_rec")
            for i, h in enumerate(heads):
                p.emit('pe', (lambda i=i: lambda e: e.matmul(
                    bank(7), onesrow_f[:, :],
                    scr2[0:1, i * 512:(i + 1) * 512],
                    start=True, stop=True))(),
                    waits=[("s_rec", t_rec), ("s_rbe", p.tot("s_rbe"))],
                    inc=("s_bc", 1))
                p.emit('dve', lambda e: e.tensor_copy(rb[:, :], bank(7)),
                       waits=[("s_bc", p.tot("s_bc")),
                              ("s_cn", p.tot("s_cn"))], inc=("s_rbe", 1))
                cslot = p.tot("s_cn") % 2
                ssem = "s_send%d" % cslot
                p.emit('dve', (lambda i=i, cslot=cslot: lambda e: e.tensor_mul(
                    ctxn[:, cslot * 512:(cslot + 1) * 512], bank(i), rb[:, :]))(),
                    waits=[("s_rbe", p.tot("s_rbe")),
                           (ssem, p.tot(ssem))],
                    inc=("s_cn", 1))
                p.emit('gp', (lambda h=h, qb=qb, cslot=cslot: lambda e: e.dma_start(
                    out=ctx_send[h * 128:(h + 1) * 128, qb * 512:(qb + 1) * 512],
                    in_=ctxn[:, cslot * 512:(cslot + 1) * 512]))(),
                    waits=[("s_cn", p.tot("s_cn"))], inc=(ssem, 16))
        att_stmm[qb] = p.tot("s_stmm")

    for t in range(8):
        emit_tile(t)
    emit_qkv(0)
    for b in range(8):
        nxt = 8 * (b + 1)
        emit_tile(nxt)
        emit_attn(2 * b)
        emit_tile(nxt + 1); emit_tile(nxt + 2)
        emit_tile(nxt + 3); emit_tile(nxt + 4)
        emit_attn(2 * b + 1)
        emit_tile(nxt + 5); emit_tile(nxt + 6); emit_tile(nxt + 7)
        if b < 7:
            emit_qkv(b + 1)

    assert cstate["ffn_out_t"] is not None
    FFN_CONV_T = cstate["ffn_out_t"]
    while cstate["idx"] < len(conv_jobs):
        emit_conv_job()
    CONV_OUT_T0 = p.tot("s_cvout0"); CONV_OUT_T1 = p.tot("s_cvout1")
    T_CVIN_ALL = p.tot("s_cvin0")
    T_XN_ALL = p.tot("s_xn")
    T_QE_ALL = p.tot("s_qe")
    T_VE_ALL = p.tot("s_ve")
    T_XNTE_ALL = p.tot("s_xnTe")

    p.emit('sp', lambda e: e.dma_start(out=aT_sb, in_=a_bf[:, :]),
           waits=[("s_cvout0", FFN_CONV_T),
                  ("s_qe", T_QE_ALL), ("s_xn", T_XN_ALL),
                  ("s_xnTe", T_XNTE_ALL),
                  ("s_tp", p.tot("s_tp"))],
           inc=("s_wbs", 16))
    p.emit('sp', lambda e: e.dma_start(out=bT_sb, in_=bT_bf[:, :]), inc=("s_wbs", 16))
    p.emit('sp', lambda e: e.dma_start(out=wrT_sb, in_=r_bf[:, :]), inc=("s_wbs", 16))
    p.emit('sp', lambda e: e.dma_start(out=bbr[0:1, 0:DM], in_=biasrows[2:3, :]),
           waits=[("s_browd", T_BROWD)], inc=("s_wbs", 16))
    p.emit('sp', lambda e: e.dma_start(out=bbr[0:1, DM:2 * DM], in_=biasrows[3:4, :]),
           inc=("s_wbs", 16))
    T_FFNW = p.tot("s_wbs")

    def mk_router(e):
        inst = None
        for k in range(32):
            for j in range(2):
                inst = e.matmul(PS[0:8, j * 512:(j + 1) * 512],
                                wrT_sb[:, k * 8:(k + 1) * 8], mv_xnT(k, j),
                                start=(k == 0), stop=(k == 31))
        return inst
    p.emit('pe', mk_router, waits=[("s_wbs", T_FFNW), ("s_xnTe", T_XNTE_ALL),
                                   ("s_qe", T_QE_ALL), ("s_ve", T_VE_ALL),
                                   ("s_cn", p.tot("s_cn")),
                                   ("s_exp", p.tot("s_exp"))],
           inc=("s_r", 1))
    p.emit('dve', lambda e: e.tensor_copy(ltT[0:8, :], PS[0:8, 0:1024]),
           waits=[("s_r", p.tot("s_r"))], inc=("s_re", 1))

    def mk_ltp(e):
        inst = None
        for tau in range(8):
            inst = e.matmul(bank(2)[0:128, tau * 8:(tau + 1) * 8],
                            ltT[0:8, tau * 128:(tau + 1) * 128], ident8f[:, :],
                            is_transpose=True, start=(tau == 0), stop=(tau == 7))
        return inst
    p.emit('pe', mk_ltp, waits=[("s_re", p.tot("s_re"))], inc=("s_rtp", 1))
    p.emit('dve', lambda e: e.tensor_copy(ltok[:, :], bank(2)[:, 0:64]),
           waits=[("s_rtp", p.tot("s_rtp"))], inc=("s_rte", 1))
    T_RTE = p.tot("s_rte")

    for tau in range(8):
        lt = ltok[:, tau * 8:(tau + 1) * 8]

        def mk_top2a(lt=lt):
            def fn(e):
                _ch("dve", e, e.reduce_max(tp2[:, 0:1], lt, axis=AX.X, negate=True))
                _ch("dve", e, e.tensor_scalar_mul(tp2[:, 1:2], tp2[:, 0:1], -1.0))
                _ch("dve", e, e.tensor_single_scalar(tp2[:, 8:16], lt, tp2[:, 1:2],
                                                     ALU.is_equal))
                _ch("dve", e, e.scalar_tensor_tensor(tp2[:, 16:24], tp2[:, 8:16],
                                                     -1e30, lt, ALU.mult, ALU.add))
                _ch("dve", e, e.reduce_max(tp2[:, 2:3], tp2[:, 16:24], axis=AX.X,
                                           negate=True))
                return e.tensor_scalar_mul(tp2[:, 3:4], tp2[:, 2:3], -1.0)
            return fn
        p.emit('dve', mk_top2a(), waits=[("s_rte", T_RTE), ("s_t2", p.tot("s_t2"))],
               inc=("s_t2", 1))
        p.emit('act', (lambda lt=lt: lambda e: e.activation(
            tp2[:, 24:32], lt, AF.Exp, bias=tp2[:, 0:1]))(),
            waits=[("s_t2", p.tot("s_t2"))], inc=("s_t2", 1))

        def mk_top2b(lt=lt, tau=tau):
            def fn(e):
                _ch("dve", e, e.tensor_single_scalar(tp2[:, 32:40], lt, tp2[:, 3:4],
                                                     ALU.is_ge))
                _ch("dve", e, e.tensor_mul(tp2[:, 24:32], tp2[:, 24:32], tp2[:, 32:40]))
                _ch("dve", e, e.reduce_sum(tp2[:, 4:5], tp2[:, 24:32], axis=AX.X))
                _ch("dve", e, e.reciprocal(tp2[:, 5:6], tp2[:, 4:5]))
                return e.tensor_scalar(cbuf[:, tau * 8:(tau + 1) * 8], tp2[:, 24:32],
                                       tp2[:, 5:6], 2.0, ALU.mult, ALU.mult)
            return fn
        p.emit('dve', mk_top2b(), waits=[("s_t2", p.tot("s_t2"))], inc=("s_t2", 1))
    T_T2 = p.tot("s_t2")

    def mk_ctp(e):
        inst = None
        for tau in range(8):
            bknum = 3 + tau // 4
            inst = e.matmul(PS[0:8, bknum * 512 + (tau % 4) * 128:
                               bknum * 512 + (tau % 4 + 1) * 128],
                            cbuf[:, tau * 8:(tau + 1) * 8], identf[:, :],
                            is_transpose=True, start=(tau % 4 == 0), stop=(tau % 4 == 3))
        return inst
    p.emit('pe', mk_ctp, waits=[("s_t2", T_T2)], inc=("s_ct", 1))
    p.emit('dve', lambda e: e.tensor_copy(ctT[0:8, 0:512], PS[0:8, 3 * 512:4 * 512]),
           waits=[("s_ct", p.tot("s_ct"))], inc=("s_cte", 1))
    p.emit('dve', lambda e: e.tensor_copy(ctT[0:8, 512:1024], PS[0:8, 4 * 512:5 * 512]),
           inc=("s_cte", 1))
    T_CTE = p.tot("s_cte")

    def mk_exp_c(e):
        e.matmul(bank(5), Ef[:, :], ctT[0:8, 0:512], start=True, stop=True)
        return e.matmul(bank(2), Ef[:, :], ctT[0:8, 512:1024], start=True, stop=True)
    p.emit('pe', mk_exp_c, waits=[("s_cte", T_CTE), ("s_rte", T_RTE)], inc=("s_cx", 1))
    p.emit('dve', lambda e: e.tensor_copy(ce[:, 0:512], bank(5)),
           waits=[("s_cx", p.tot("s_cx"))], inc=("s_cxe", 1))
    p.emit('dve', lambda e: e.tensor_copy(ce[:, 512:1024], bank(2)), inc=("s_cxe", 1))
    T_CXE = p.tot("s_cxe")

    def mk_h(e):
        inst = None
        for k in range(32):
            for j in range(2):
                inst = e.matmul(bank(6 + j), aT_sb[:, k * 128:(k + 1) * 128],
                                mv_xnT(k, j), start=(k == 0), stop=(k == 31))
        return inst
    p.emit('pe', mk_h, waits=[("s_wbs", T_FFNW), ("s_xnTe", T_XNTE_ALL),
                              ("s_ve", T_VE_ALL), ("s_rec", p.tot("s_rec")),
                              ("s_rbe", p.tot("s_rbe"))], inc=("s_h", 1))
    p.emit('dve', lambda e: e.tensor_mul(hw[:, :], bank(6, 2), ce[:, :]),
           waits=[("s_h", p.tot("s_h")), ("s_cxe", T_CXE)], inc=("s_hw", 1))
    T_HW = p.tot("s_hw")

    for tp_i in range(4):
        for half in range(2):
            for k in range(33):
                if k < 32:
                    slot = k % 3
                    wsem = "s_wst%d" % slot
                    p.emit('sp', (lambda k=k, half=half, slot=slot: lambda e: e.dma_start(
                        out=wbstream[slot],
                        in_=wb_bf[k * 128:(k + 1) * 128, half * 2048:(half + 1) * 2048]))(),
                        waits=[("s_base", p.tot("s_base") - (0 if k < 3 else 2)),
                               ("s_cvout0", FFN_CONV_T)],
                        inc=(wsem, 16))
                    t_w = p.tot(wsem)

                def mk_base(k=k, tp_i=tp_i, half=half):
                    def fn(e):
                        inst = None
                        for ti in range(2):
                            tau = tp_i * 2 + ti
                            if k < 32:
                                lhsT = xnT[:, tau * 4096 + k * 128:
                                           tau * 4096 + (k + 1) * 128]
                                mvr = wbstream[k % 3]
                            else:
                                lhsT = hw[:, tau * 128:(tau + 1) * 128]
                                mvr = bT_sb[:, half * 2048:(half + 1) * 2048]
                            for j in range(4):
                                inst = e.matmul(bank(ti * 4 + j), lhsT,
                                                mvr[:, j * 512:(j + 1) * 512],
                                                start=(k == 0), stop=False)
                        return inst
                    return fn
                wl = [(wsem, t_w)] if k < 32 else [("s_hw", T_HW)]
                if k == 0:
                    wl += [("s_hw", T_HW), ("s_cxe", T_CXE), ("s_cte", T_CTE),
                           ("s_re", p.tot("s_re")), ("s_rte", T_RTE),
                           ("s_g", p.tot("s_g"))]
                p.emit('pe', mk_base(), waits=wl,
                       inc=("s_base", 1) if k < 32 else None)

            def mk_bbias():
                def fn(e):
                    inst = None
                    for ti in range(2):
                        for j in range(4):
                            col = half * 2048 + j * 512
                            e.matmul(bank(ti * 4 + j), onesrow_b[:, :],
                                     bbr[0:1, col:col + 512], start=False, stop=False)
                            inst = e.matmul(bank(ti * 4 + j), onesrow_b[:, :],
                                            bbr[0:1, DM + col:DM + col + 512],
                                            start=False, stop=True)
                    return inst
                return fn
            p.emit('pe', mk_bbias(), waits=[("s_wbs", T_FFNW)], inc=("s_bmm", 1))
            t_bs = p.tot("s_bmm")
            for ti in range(2):
                tau = tp_i * 2 + ti
                gslot = gstage[ti]
                gsem = "s_gw%d" % ti
                p.emit('act', (lambda ti=ti, gslot=gslot: lambda e: e.activation(
                    gslot, bank(ti * 4, 4), AF.Silu))(),
                    waits=[("s_bmm", t_bs), (gsem, p.tot(gsem)),
                           ("s_cvout0", CONV_OUT_T0),
                           ("s_cvin0", T_CVIN_ALL)],
                    inc=("s_g", 1))
                p.emit('gp', (lambda tau=tau, half=half, gslot=gslot: lambda e: e.dma_start(
                    out=g_dram[tau * 128:(tau + 1) * 128, half * 2048:(half + 1) * 2048],
                    in_=gslot))(),
                    waits=[("s_g", p.tot("s_g"))], inc=(gsem, 16))
    T_G_ALL0 = p.tot("s_gw0"); T_G_ALL1 = p.tot("s_gw1")
    T_SILU = p.tot("s_g")
    T_BMM = p.tot("s_bmm")

    if DBG:
        p.emit('gp', lambda e: e.dma_start(out=dbg["wbf"], in_=w_bf),
               waits=[("s_cvout0", CONV_OUT_T0), ("s_cvout1", CONV_OUT_T1)],
               inc=("s_dbg", 16))
        p.emit('gp', lambda e: e.dma_start(out=dbg["abf"], in_=a_bf),
               waits=[("s_cvout0", CONV_OUT_T0), ("s_cvout1", CONV_OUT_T1)],
               inc=("s_dbg", 16))
        p.emit('gp', lambda e: e.dma_start(out=dbg["xnT"], in_=xnT),
               waits=[("s_xnTe", T_XNTE_ALL)], inc=("s_dbg", 16))
        p.emit('gp', lambda e: e.dma_start(out=dbg["Q"], in_=Qsb[:, :]),
               waits=[("s_qe", T_QE_ALL)], inc=("s_dbg", 16))
        p.emit('gp', lambda e: e.dma_start(out=dbg["K"], in_=Ksb[:, :]),
               inc=("s_dbg", 16))
        p.emit('gp', lambda e: e.dma_start(out=dbg["V"], in_=Vsb[:, :]),
               waits=[("s_ve", T_VE_ALL)], inc=("s_dbg", 16))
        p.emit('gp', lambda e: e.dma_start(out=dbg["hw"], in_=hw[:, :]),
               waits=[("s_hw", T_HW)], inc=("s_dbg", 16))
        p.emit('gp', lambda e: e.dma_start(out=dbg["ce"], in_=ce[:, :]),
               waits=[("s_cxe", T_CXE)], inc=("s_dbg", 16))
        p.emit('gp', lambda e: e.dma_start(out=dbg["lt"], in_=ltok[:, :]),
               waits=[("s_rte", T_RTE)], inc=("s_dbg", 16))

    T_SEND0 = p.tot("s_send0"); T_SEND1 = p.tot("s_send1")
    if not PHASE1_ONLY:
        p.emit('gp', lambda e: e.collective_compute(
            "AllGather", ALU.bypass, replica_groups=[list(range(NC))],
            ins=[ctx_send.opt()], outs=[gath.opt()]),
            waits=[("s_send0", T_SEND0), ("s_send1", T_SEND1)], inc=("s_cc", 1))


        def mk_gathread(e):
            pid = e.partition_id()
            g3 = gath.rearrange("(k p) (n t) -> n p k t", p=128, t=1024)
            return e.dma_start(out=ctxr, in_=g3[pid])
        p.emit('sp', mk_gathread, waits=[("s_cc", 1), ("s_gw0", T_G_ALL0),
                                     ("s_gw1", T_G_ALL1),
                                     ("s_dbg", p.tot("s_dbg"))], inc=("s_ctxr", 16))
        p.emit('sp', lambda e: e.dma_start(out=bor[0:1, 0:DM], in_=biasrows[0:1, :]),
               inc=("s_ctxr", 16))
        p.emit('sp', lambda e: e.dma_start(out=bor[0:1, DM:2 * DM], in_=biasrows[1:2, :]),
               inc=("s_ctxr", 16))
        T_CTXR = p.tot("s_ctxr")

        for tp_i in range(4):
            for half in range(2):
                for k in range(32):
                    slot = k % 4
                    wsem = "s_wst%d" % slot
                    p.emit('sp', (lambda k=k, half=half, slot=slot: lambda e: e.dma_start(
                        out=wostream[slot],
                        in_=wo_bf[k * 128:(k + 1) * 128, half * 2048:(half + 1) * 2048]))(),
                        waits=[("s_wmm", p.tot("s_wmm") - (0 if k < 4 else 3)),
                               ("s_cvout0", CONV_OUT_T0)],
                        inc=(wsem, 16))
                    t_w = p.tot(wsem)

                    def mk_wo(k=k, tp_i=tp_i, slot=slot):
                        def fn(e):
                            inst = None
                            for ti in range(2):
                                tau = tp_i * 2 + ti
                                lhsT = ctxr[:, k * 1024 + tau * 128: k * 1024 + (tau + 1) * 128]
                                for j in range(4):
                                    inst = e.matmul(bank(ti * 4 + j), lhsT,
                                                    wostream[slot][:, j * 512:(j + 1) * 512],
                                                    start=(k == 0), stop=False)
                            return inst
                        return fn
                    wl = [(wsem, t_w)]
                    if k == 0:
                        wl += [("s_ctxr", T_CTXR), ("s_ev1", p.tot("s_ev1")),
                               ("s_g", T_SILU)]
                    p.emit('pe', mk_wo(), waits=wl, inc=("s_wmm", 1))

                def mk_wobias(half=half):
                    def fn(e):
                        inst = None
                        for ti in range(2):
                            for j in range(4):
                                col = half * 2048 + j * 512
                                e.matmul(bank(ti * 4 + j), onesrow_b[:, :],
                                         bor[0:1, col:col + 512], start=False, stop=False)
                                inst = e.matmul(bank(ti * 4 + j), onesrow_b[:, :],
                                                bor[0:1, DM + col:DM + col + 512],
                                                start=False, stop=True)
                        return inst
                    return fn
                p.emit('pe', mk_wobias(), waits=[("s_ctxr", T_CTXR)], inc=("s_bmm", 1))
                t_wod = p.tot("s_bmm")
                for ti in range(2):
                    tau = tp_i * 2 + ti
                    p.emit('gp', (lambda tau=tau, half=half, ti=ti: lambda e: e.dma_start(
                        out=ghalf[ti], in_=g_dram[tau * 128:(tau + 1) * 128,
                                                  half * 2048:(half + 1) * 2048]))(),
                        waits=[("s_gw0", T_G_ALL0), ("s_gw1", T_G_ALL1),
                               ("s_ev1", p.tot("s_ev1")),
                               ("s_dbg", p.tot("s_dbg"))],
                        inc=("s_gh", 16))
                    p.emit('sp', (lambda tau=tau, half=half, ti=ti: lambda e: e.dma_start(
                        out=xohalf[ti], in_=xown[tau * 128:(tau + 1) * 128,
                                                 half * 2048:(half + 1) * 2048]))(),
                        waits=[("s_ev1", p.tot("s_ev1")),
                               ("s_dbg", p.tot("s_dbg"))],
                        inc=("s_xo", 16))
                    t_gh = p.tot("s_gh"); t_xo = p.tot("s_xo")

                    def mk_fin(ti=ti):
                        def fn(e):
                            _ch("dve", e, e.tensor_mul(ohalf[ti], bank(ti * 4, 4),
                                                       ghalf[ti]))
                            return e.tensor_add(ohalf[ti], ohalf[ti], xohalf[ti])
                        return fn
                    p.emit('dve', mk_fin(), waits=[("s_bmm", t_wod), ("s_gh", t_gh),
                                                   ("s_xo", t_xo),
                                                   ("s_out", p.tot("s_out"))],
                           inc=("s_ev1", 1))
                    p.emit('gp', (lambda tau=tau, half=half, ti=ti: lambda e: e.dma_start(
                        out=out_ext[tau * 128:(tau + 1) * 128,
                                    half * 2048:(half + 1) * 2048],
                        in_=ohalf[ti]))(),
                        waits=[("s_ev1", p.tot("s_ev1"))], inc=("s_out", 16))
        if DBG:
            p.emit('gp', lambda e: e.dma_start(out=dbg["scr2"][0:1, :], in_=scr2[0:1, :]),
                   waits=[("s_cc", 1)], inc=("s_dbg", 16))
            p.emit('gp', lambda e: e.dma_start(out=dbg["send"], in_=ctx_send),
                   waits=[("s_cc", 1)], inc=("s_dbg", 16))
            p.emit('gp', lambda e: e.dma_start(out=dbg["ctxr"], in_=ctxr[:, :]),
                   waits=[("s_ctxr", T_CTXR)], inc=("s_dbg", 16))
        T_OUT = p.tot("s_out")
        p.emit('gp', None, waits=[("s_out", T_OUT), ("s_dbg", p.tot("s_dbg"))])

    else:
        p.emit('gp', lambda e: e.dma_start(out=out_ext[0:128, :],
                                           in_=Qsb[:, 0:8192].bitcast(F32)),
               waits=[("s_qe", T_QE_ALL)], inc=("s_out", 16))
        p.emit('gp', None, waits=[("s_out", p.tot("s_out")),
                                  ("s_dbg", p.tot("s_dbg")),
                                  ("s_cvout0", CONV_OUT_T0),
                                  ("s_cvout1", CONV_OUT_T1),
                                  ("s_gw0", T_G_ALL0), ("s_gw1", T_G_ALL1)])

    semnames = sorted({nm for (_, ws, _, inc, _t) in p.ops
                       for nm in ([w[0] for w in ws] + ([inc[0]] if inc else []))})
    import contextlib
    stack = contextlib.ExitStack()
    sems = {}

    with nc.Block() as block:
        for sn in semnames:
            sems[sn] = stack.enter_context(nc.semaphore(sn))
        for key in ("dve", "act", "gp"):
            _CHSEMS[key] = stack.enter_context(nc.semaphore("s_ch_" + key))
        _CH.update({"dve": 0, "act": 0, "gp": 0})

        def runner(engine_key):
            def run(eng):
                for (ek, ws, fn, inc, _tag) in p.ops:
                    if ek != engine_key:
                        continue
                    for (sn, v) in ws:
                        eng.wait_ge(sems[sn], v)
                    if fn is None:
                        continue
                    inst = fn(eng)
                    if inc is not None:
                        assert inst is not None
                        inst.then_inc(sems[inc[0]], inc[1])
            return run

        block.tensor(runner('pe'))
        block.scalar(runner('act'))
        block.vector(runner('dve'))
        block.gpsimd(runner('gp'))
        block.sync(runner('sp'))

    if return_plan:
        return nc, p
    return nc


_NC_CACHE = {}


def _host_inputs(inputs):
    x = np.ascontiguousarray(np.asarray(inputs["L0_attn_hidden_states"],
                                        dtype=np.float32).reshape(S, DM))
    Wq = np.asarray(inputs["L0_attn_Wq"], np.float32)
    bq = np.asarray(inputs["L0_attn_bq"], np.float32)
    Wk = np.asarray(inputs["L0_attn_Wk"], np.float32)
    bk = np.asarray(inputs["L0_attn_bk"], np.float32)
    Wv = np.asarray(inputs["L0_attn_Wv"], np.float32)
    bv = np.asarray(inputs["L0_attn_bv"], np.float32)
    Wo = np.ascontiguousarray(np.asarray(inputs["L0_attn_Wo"], np.float32))
    bo_ = np.asarray(inputs["L0_attn_bo"], np.float32)
    Wb = np.asarray(inputs["L0_ffn_W_base"], np.float32)
    bb_ = np.asarray(inputs["L0_ffn_b_base"], np.float32)
    Wr = np.asarray(inputs["L0_ffn_W_router"], np.float32)
    A = np.asarray(inputs["L0_ffn_A"], np.float32)
    B = np.asarray(inputs["L0_ffn_B"], np.float32)

    j = np.arange(64, dtype=np.float32) / np.float32(64.0)
    inv = (np.float32(1.0) / np.power(np.float32(10000.0), j)).astype(np.float32)
    pos = np.arange(S, dtype=np.float32)
    ang = (pos[:, None] * inv[None, :]).astype(np.float32)
    cosT = np.cos(ang).T.astype(np.float32)
    sinT = np.sin(ang).T.astype(np.float32)
    csA = np.concatenate([cosT, sinT], axis=0)
    csB = np.concatenate([sinT, cosT], axis=0)
    cs_tab = np.ascontiguousarray(np.concatenate([csA, csB], axis=1).astype(np.float32))
    swapm_np = np.zeros((128, 128), np.float32)
    for i in range(128):
        swapm_np[i, (i + 64) % 128] = 1.0

    mask_np = np.zeros((128, 4 * 512), dtype=ml_dtypes.bfloat16)
    for c in range(4):
        kk = np.arange(128)[:, None] + 128 * c
        qq = np.arange(512)[None, :]
        mc = (kk <= qq).astype(ml_dtypes.bfloat16)
        mask_np[:, c * 512:(c + 1) * 512] = mc

    Ef_np = np.zeros((8, 128), np.float32)
    for e in range(8):
        Ef_np[e, e * 16:(e + 1) * 16] = 1.0
    cvec_np = np.zeros((128, 4), np.float32)
    cvec_np[:, 0] = SHIFT; cvec_np[:, 1] = GAMMA; cvec_np[:, 2] = EPS

    common = dict(
        x=x, wo=Wo,
        bo=np.ascontiguousarray(bo_.reshape(1, DM)),
        wbT=np.ascontiguousarray(Wb.T),
        bb=np.ascontiguousarray(bb_.reshape(1, DM)),
        wrT=np.ascontiguousarray(Wr.T),
        aT=np.ascontiguousarray(A.reshape(128, DM).T),
        bTm=np.ascontiguousarray(B.transpose(0, 2, 1).reshape(128, DM)),
        cs=cs_tab, masks=mask_np,
        identb=np.eye(128, dtype=ml_dtypes.bfloat16),
        identf=np.eye(128, dtype=np.float32),
        ident8f=np.eye(8, dtype=np.float32),
        onescol=np.ones((128, 1), ml_dtypes.bfloat16),
        onesrow_b=np.ones((1, 128), ml_dtypes.bfloat16),
        onesrow_f=np.ones((1, 128), np.float32),
        Ef=Ef_np, cvec=cvec_np, swapm=swapm_np,
    )
    in_maps = []
    for g in range(NC):
        m = dict(common)
        m.update(
            xown=np.ascontiguousarray(x[g * OWN:(g + 1) * OWN, :]),
            wq=np.ascontiguousarray(Wq[:, g * 512:(g + 1) * 512]),
            wk=np.ascontiguousarray(Wk[:, g * 128:(g + 1) * 128]),
            wv=np.ascontiguousarray(Wv[:, g * 128:(g + 1) * 128]),
            bq4=np.ascontiguousarray(bq[g * 512:(g + 1) * 512].reshape(4, 128).T),
            bk1=np.ascontiguousarray(bk[g * 128:(g + 1) * 128].reshape(1, 128).T),
            bv1=np.ascontiguousarray(bv[g * 128:(g + 1) * 128].reshape(1, 128).T),
        )
        in_maps.append(m)
    return in_maps


def kernel(**inputs):
    in_maps = _host_inputs(inputs)
    if "nc" not in _NC_CACHE:
        _NC_CACHE["nc"] = build_program()
    nc = _NC_CACHE["nc"]
    res = run_bass_kernel_spmd(nc, in_maps, core_ids=list(range(NC)))
    out = np.concatenate([res.results[g]["out"] for g in range(NC)], axis=0)
    return np.ascontiguousarray(out.reshape(1, S, DM))


# version 15 (final)
# speedup vs baseline: 1.0598x; 1.0527x over previous
import math
import os
import numpy as np
import ml_dtypes

import concourse.bass as bass
import concourse.mybir as mybir
from concourse.bass_utils import run_bass_kernel_spmd

F32 = mybir.dt.float32
BF16 = mybir.dt.bfloat16
AF = mybir.ActivationFunctionType
ALU = mybir.AluOpType
AX = mybir.AxisListType

NC = 8
S = 8192
DM = 4096
HQ = 4
OWN = 1024
TPB = 8
NQB = 16
GAMMA = 1.0 / math.sqrt(128.0)
SHIFT = -20.0
EPS = 1e-5




_CH = {"dve": 0, "act": 0, "gp": 0}
_CHSEMS = {}


def _ch(key, e, inst):
    _CH[key] += 1
    e_sem = _CHSEMS[key]
    inst.then_inc(e_sem, 1)
    e.wait_ge(e_sem, _CH[key])
    return inst


def _mk_rope(e_qtmp, e_swp, e_cs, e_tmp, dsts):
    def fn(e):
        inst = None
        for j in range(2):
            c0, c1 = j * 512, (j + 1) * 512
            tA = e_tmp[0:64, 0:512]; tB = e_tmp[0:64, 512:1024]
            tC = e_tmp[64:128, 0:512]; tD = e_tmp[64:128, 512:1024]
            e.tensor_mul(tA, e_qtmp[0:64, c0:c1], e_cs[0:64, c0:c1])
            _ch("dve", e, e.tensor_mul(tB, e_swp[0:64, c0:c1],
                                       e_cs[0:64, 1024 + c0:1024 + c1]))
            e.tensor_sub(dsts[0][:, c0:c1], tA, tB)
            e.tensor_mul(tC, e_swp[64:128, c0:c1], e_cs[64:128, c0:c1])
            _ch("dve", e, e.tensor_mul(tD, e_qtmp[64:128, c0:c1],
                                       e_cs[64:128, 1024 + c0:1024 + c1]))
            inst = e.tensor_add(dsts[1][:, c0:c1], tC, tD)
        return inst
    return fn

class Plan:

    def __init__(self):
        self.ops = []
        self.cnt = {}

    label = ""

    def emit(self, engine, fn, waits=(), inc=None):
        ws = [(s, v) for (s, v) in waits if v > 0]
        if inc is not None:
            self.cnt[inc[0]] = self.cnt.get(inc[0], 0) + inc[1]
        self.ops.append((engine, ws, fn, inc, Plan.label))

    def tot(self, sem):
        return self.cnt.get(sem, 0)


def build_program(return_plan=False):
    nc = bass.Bass("TRN2", target_bir_lowering=False,
                   detect_race_conditions=not os.environ.get("KMK_NORACE"))

    def ext(name, shape, dtype=F32):
        return nc.dram_tensor(name, shape, dtype, kind="ExternalInput").ap()

    x = ext("x", [S, DM])
    xown = ext("xown", [OWN, DM])
    wq = ext("wq", [DM, 512]); wk = ext("wk", [DM, 128]); wv = ext("wv", [DM, 128])
    bq4 = ext("bq4", [128, 4]); bk1 = ext("bk1", [128, 1]); bv1 = ext("bv1", [128, 1])
    wo = ext("wo", [DM, DM]); bo = ext("bo", [1, DM])
    wbT = ext("wbT", [DM, DM]); bb = ext("bb", [1, DM])
    wrT = ext("wrT", [DM, 8])
    aT = ext("aT", [DM, 128])
    bTm = ext("bTm", [128, DM])
    cs = ext("cs", [128, 2 * S])
    swapm_d = ext("swapm", [128, 128], F32)
    masks_d = ext("masks", [128, 4 * 1024], BF16)
    identb_d = ext("identb", [128, 128], BF16)
    ident8f_d = ext("ident8f", [8, 8], F32)
    identf_d = ext("identf", [128, 128], F32)
    onescol_d = ext("onescol", [128, 1], BF16)
    onesrow_b_d = ext("onesrow_b", [1, 128], BF16)
    onesrow_f_d = ext("onesrow_f", [1, 128], F32)
    Ef_d = ext("Ef", [8, 128], F32)
    cvec_d = ext("cvec", [128, 4], F32)

    out_ext = nc.dram_tensor("out", [OWN, DM], F32, kind="ExternalOutput").ap()

    w_bf = nc.dram_tensor("w_bf", [DM, 768], BF16).ap()
    wo_bf = nc.dram_tensor("wo_bf", [DM, DM], BF16).ap()
    wb_bf = nc.dram_tensor("wb_bf", [DM, DM], BF16).ap()
    a_bf = nc.dram_tensor("a_bf", [128, DM], BF16).ap()
    r_bf = nc.dram_tensor("r_bf", [128, 256], BF16).ap()
    bT_bf = nc.dram_tensor("bT_bf", [128, DM], BF16).ap()
    q_spill = nc.dram_tensor("q_spill", [512, 6144], BF16).ap()
    ctx_send = nc.dram_tensor("ctx_send", [512, S], BF16).ap()
    gath = nc.dram_tensor("gath", [NC * 512, S], BF16, addr_space="Shared").ap()
    DBG = bool(os.environ.get("KMK_DEBUG"))
    dbgkind = "ExternalOutput" if DBG else "Internal"
    g_dram = nc.dram_tensor("g_dram", [OWN, DM], F32, kind=dbgkind).ap()
    biasrows = nc.dram_tensor("biasrows", [4, DM], BF16).ap()
    dbg = {}
    if DBG:
        dbg["xnT"] = nc.dram_tensor("dbg_xnT", [128, TPB * DM], BF16, kind="ExternalOutput").ap()
        dbg["Q"] = nc.dram_tensor("dbg_Q", [128, HQ * 2048], BF16, kind="ExternalOutput").ap()
        dbg["K"] = nc.dram_tensor("dbg_K", [128, S], BF16, kind="ExternalOutput").ap()
        dbg["V"] = nc.dram_tensor("dbg_V", [128, S], BF16, kind="ExternalOutput").ap()
        dbg["hw"] = nc.dram_tensor("dbg_hw", [128, 1024], BF16, kind="ExternalOutput").ap()
        dbg["ce"] = nc.dram_tensor("dbg_ce", [128, 1024], F32, kind="ExternalOutput").ap()
        dbg["lt"] = nc.dram_tensor("dbg_lt", [128, 64], F32, kind="ExternalOutput").ap()
        dbg["scr2"] = nc.dram_tensor("dbg_scr2", [128, 2048], F32, kind="ExternalOutput").ap()
        dbg["ctxr"] = nc.dram_tensor("dbg_ctxr", [128, 32 * 1024], BF16, kind="ExternalOutput").ap()
        dbg["send"] = nc.dram_tensor("dbg_send", [512, S], BF16, kind="ExternalOutput").ap()
        dbg["wbf"] = nc.dram_tensor("dbg_wbf", [DM, 768], BF16, kind="ExternalOutput").ap()
        dbg["qtmp"] = nc.dram_tensor("dbg_qtmp", [128, 1024], F32, kind="ExternalOutput").ap()
        dbg["ps03"] = nc.dram_tensor("dbg_ps03", [128, 2048], F32, kind="ExternalOutput").ap()
        dbg["kps"] = nc.dram_tensor("dbg_kps", [128, 1024], F32, kind="ExternalOutput").ap()
        dbg["abf"] = nc.dram_tensor("dbg_abf", [128, DM], BF16, kind="ExternalOutput").ap()

    p = Plan()

    def sbuf(name, shape, dtype):
        return nc.sbuf_tensor(name, shape, dtype).__enter__()

    Qsb = sbuf("Qsb", [128, HQ * 2048], BF16)
    Ksb = sbuf("Ksb", [128, S], BF16)
    Vsb = sbuf("Vsb", [128, S], BF16)
    identb = sbuf("identb_s", [128, 128], BF16)
    identf = sbuf("identf_s", [128, 128], F32)
    ident8f = sbuf("ident8f_s", [8, 8], F32)
    onescol = sbuf("onescol_s", [128, 1], BF16)
    onesrow_b = sbuf("onesrow_b_s", [1, 128], BF16)
    onesrow_f = sbuf("onesrow_f_s", [1, 128], F32)
    Ef = sbuf("Ef_s", [8, 128], F32)
    cvec = sbuf("cvec_s", [128, 4], F32)
    swapm = sbuf("swapm_s", [128, 128], F32)
    ctxn = sbuf("ctxn", [128, 512], BF16)

    ARENA_B = 148 * 1024
    arena = sbuf("arena", [128, ARENA_B], mybir.dt.uint8)

    def carve(off_b, cols, dtype):
        esz = mybir.dt.size(dtype)
        return arena[:, off_b:off_b + cols * esz].bitcast(dtype)

    o = 0
    xb = [carve(o, DM, F32), carve(o + DM * 4, DM, F32)]; o += 2 * DM * 4
    xn2 = [carve(o, DM, BF16), carve(o + DM * 2, DM, BF16)]; o += DM * 4
    cs_sb = carve(o, 2048, F32); o += 8192
    qtmp = carve(o, 1024, F32); o += 4096
    ropet = carve(o, 1024, F32); o += 4096
    vstage = carve(o, 1024, BF16); o += 2048
    qstage = carve(o, 1024, BF16); o += 2048
    stt = carve(o, 32, F32); o += 128
    wsb = [carve(o, 768, BF16), carve(o + 768 * 2, 768, BF16)]; o += 2 * 768 * 2
    cvst = [carve(o, 1024, F32), carve(o + 4096, 1024, F32)]; o += 8192
    cvst_b = [carve(o, 1024, BF16), carve(o + 2048, 1024, BF16)]; o += 4096
    bqs = carve(o, 4, F32); o += 16
    bks = carve(o, 1, F32); o += 4
    bvs = carve(o, 1, F32); o += 4
    P1_LN_END = o
    assert o <= 84 * 1024, o
    XNT_OFF = 84 * 1024
    xnT = carve(XNT_OFF, TPB * DM, BF16)
    assert XNT_OFF + TPB * DM * 2 <= ARENA_B
    xnTv = xnT.rearrange("p (tt kk c) -> p tt kk c", tt=8, kk=32, c=128)

    def mv_xnT(k, j):
        return xnTv[:, 4 * j:4 * j + 4, k, :]

    o = 0
    wbstream = [carve(o + i * 4096, 2048, BF16) for i in range(3)]; o += 3 * 4096
    aT_sb = carve(o, DM, BF16); o += DM * 2
    bT_sb = carve(o, DM, BF16); o += DM * 2
    wrT_sb = carve(o, 256, BF16); o += 512
    ltT = carve(o, 1024, F32); o += 4096
    ctT = carve(o, 1024, F32); o += 4096
    ce = carve(o, 1024, F32); o += 4096
    ltok = carve(o - 4096, 64, F32)
    cbuf = carve(o - 4096 + 256, 64, F32)
    tp2 = carve(o - 4096 + 512, 64, F32)
    hw = carve(o, 1024, BF16); o += 2048
    bbr = carve(o, 2 * DM, BF16); o += DM * 4
    gstage = [carve(o, 2048, F32), carve(o + 8192, 2048, F32)]; o += 16384
    assert o <= 84 * 1024, o

    o = 0
    PTs = carve(o, 4 * 512, BF16); o += 4 * 512 * 2
    mask_sb = carve(o, 4 * 1024, BF16); o += 4 * 1024 * 2
    scr2 = carve(o, 2048, F32); o += 8192
    rb = carve(o, 512, F32); o += 2048
    dtmp = carve(o, 1024, BF16); o += 2048
    assert o <= XNT_OFF

    o = 0
    ctxr = carve(o, 32 * 1024, BF16); o += 32 * 1024 * 2
    wostream = [carve(o + i * 4096, 2048, BF16) for i in range(4)]; o += 4 * 4096
    ghalf = [carve(o, 2048, F32), carve(o + 8192, 2048, F32)]; o += 16384
    xohalf = [carve(o, 2048, F32), carve(o + 8192, 2048, F32)]; o += 16384
    ohalf = [carve(o, 2048, F32), carve(o + 8192, 2048, F32)]; o += 16384
    bor = carve(o, 2 * DM, BF16); o += DM * 4
    assert o <= ARENA_B

    PS = nc.psum_tensor("PS", [128, 4096], F32).__enter__()

    def bank(b, w=1):
        return PS[:, b * 512:(b + w) * 512]

    def bankb16(b):
        return PS[:, b * 512:(b + 1) * 512].bitcast(BF16)

    SHIFT_AP = cvec[:, 0:1]; GAMMA_AP = cvec[:, 1:2]; EPS_AP = cvec[:, 2:3]

    pre_list = [
        (cvec[:, :], cvec_d), (identb[:, :], identb_d), (identf[:, :], identf_d),
        (ident8f[:, :], ident8f_d), (onescol[:, :], onescol_d),
        (onesrow_b[:, :], onesrow_b_d), (onesrow_f[:, :], onesrow_f_d),
        (Ef[:, :], Ef_d), (swapm[:, :], swapm_d),
        (bqs[:, :], bq4), (bks[:, :], bk1), (bvs[:, :], bv1),
    ]
    for dst, src in pre_list:
        p.emit('sp', (lambda dst=dst, src=src: lambda e: e.dma_start(out=dst, in_=src))(),
               inc=("s_pre", 16))
    PRE_T = p.tot("s_pre")

    for (srcrow, r0) in ((bo, 0), (bb, 2)):
        p.emit('sp', (lambda srcrow=srcrow: lambda e: e.dma_start(
            out=xb[0][0:1, :], in_=srcrow))(),
            waits=[("s_brow", p.tot("s_brow"))], inc=("s_browh", 16))
        p.emit('gp', lambda e: e.tensor_copy(xn2[0][0:1, :], xb[0][0:1, :]),
               waits=[("s_browh", p.tot("s_browh")),
                      ("s_browd", p.tot("s_browd"))], inc=("s_brow", 1))
        p.emit('gp', (lambda r0=r0: lambda e: e.dma_start(
            out=biasrows[r0:r0 + 1, :], in_=xn2[0][0:1, :]))(),
            waits=[("s_brow", p.tot("s_brow"))], inc=("s_browd", 16))
        p.emit('dve', lambda e: e.tensor_sub(xb[1][0:1, :], xb[0][0:1, :], xn2[0][0:1, :]),
               waits=[("s_brow", p.tot("s_brow")), ("s_browd", p.tot("s_browd"))],
               inc=("s_brow", 1))
        p.emit('gp', lambda e: e.tensor_copy(xn2[0][0:1, :], xb[1][0:1, :]),
               waits=[("s_brow", p.tot("s_brow")), ("s_browd", p.tot("s_browd"))],
               inc=("s_brow", 1))
        p.emit('gp', (lambda r0=r0: lambda e: e.dma_start(
            out=biasrows[r0 + 1:r0 + 2, :], in_=xn2[0][0:1, :]))(),
            waits=[("s_brow", p.tot("s_brow"))], inc=("s_browd", 16))
    T_BROW = p.tot("s_brow")
    T_BROWD = p.tot("s_browd")

    conv_jobs = [("wqkv", k) for k in range(32)]
    conv_jobs += [("wo", (k, c)) for k in range(32) for c in range(4)]
    conv_jobs += [("wb", (k, c)) for k in range(32) for c in range(4)]
    conv_jobs += [("aT", c) for c in range(4)] + [("rT", 0)] + [("bT", c) for c in range(4)]
    N_WQKV_JOBS = 32

    cstate = {"unit": 0, "idx": 0, "wqkv_out_t": None}

    def conv_aps(job):
        kind, arg = job
        if kind == "wqkv":
            k = arg
            return [(wq[k * 128:(k + 1) * 128, :], 0, 512),
                    (wk[k * 128:(k + 1) * 128, :], 512, 128),
                    (wv[k * 128:(k + 1) * 128, :], 640, 128)], \
                w_bf[k * 128:(k + 1) * 128, :], 768
        if kind == "wo":
            k, c = arg
            return [(wo[k * 128:(k + 1) * 128, c * 1024:(c + 1) * 1024], 0, 1024)], \
                wo_bf[k * 128:(k + 1) * 128, c * 1024:(c + 1) * 1024], 1024
        if kind == "wb":
            k, c = arg
            return [(wbT[k * 128:(k + 1) * 128, c * 1024:(c + 1) * 1024], 0, 1024)], \
                wb_bf[k * 128:(k + 1) * 128, c * 1024:(c + 1) * 1024], 1024
        if kind == "aT":
            c = arg
            srcv = aT.rearrange("(a p) c -> p a c", p=128)
            return [(srcv[:, c * 8:(c + 1) * 8, :], 0, 1024)], \
                a_bf[:, c * 1024:(c + 1) * 1024], 1024
        if kind == "rT":
            srcv = wrT.rearrange("(a p) c -> p a c", p=128)
            return [(srcv, 0, 256)], r_bf[:, :], 256
        if kind == "bT":
            c = arg
            return [(bTm[:, c * 1024:(c + 1) * 1024], 0, 1024)], \
                bT_bf[:, c * 1024:(c + 1) * 1024], 1024

    def emit_conv_job():
        if cstate["idx"] >= len(conv_jobs):
            return
        job = conv_jobs[cstate["idx"]]
        cstate["idx"] += 1
        segs, dst, width = conv_aps(job)
        cu = cstate["unit"]
        cstate["unit"] += 1
        slot = cu % 2
        csem = "s_cvin%d" % slot
        for (sap, off, w) in segs:
            p.emit('sp', (lambda sap=sap, off=off, w=w, slot=slot:
                          lambda e: e.dma_start(out=cvst[slot][:, off:off + w], in_=sap))(),
                   waits=[("s_cv", max(0, cu - 1))],
                   inc=(csem, 16))
        t_in = p.tot(csem)
        osem = "s_cvout%d" % (cu % 2)
        p.emit('gp', (lambda width=width, slot=slot, cu=cu:
                      lambda e: e.tensor_copy(cvst_b[cu % 2][:, 0:width],
                                              cvst[slot][:, 0:width]))(),
               waits=[(csem, t_in), (osem, p.tot(osem))],
               inc=("s_cv", 1))
        p.emit('gp', (lambda dst=dst, width=width, cu=cu:
                      lambda e: e.dma_start(out=dst, in_=cvst_b[cu % 2][:, 0:width]))(),
               waits=[("s_cv", p.tot("s_cv"))], inc=(osem, 16))
        if cstate["idx"] == N_WQKV_JOBS:
            cstate["wqkv_out_t"] = (p.tot("s_cvout0"), p.tot("s_cvout1"))

    def trickle(n):
        for _ in range(n):
            emit_conv_job()

    if os.environ.get("KMK_CONVFIRST"):
        while cstate["idx"] < len(conv_jobs):
            emit_conv_job()

    NT = 72
    woff = {0: 0, 1: 128, 2: 256, 3: 384, 4: 512, 5: 640}

    for t in range(NT):
        b = t // TPB
        own = (b == 8)
        src = (xown[(t - 64) * 128:(t - 63) * 128, :] if own
               else x[t * 128:(t + 1) * 128, :])
        xslot = xb[t % 2]
        xsem = "s_x%d" % (t % 2)
        if os.environ.get("KMK_CONVFIRST") and t == 0:
            p.emit('sp', None, waits=[("s_cvout0", p.tot("s_cvout0")),
                                      ("s_cvout1", p.tot("s_cvout1"))])
        p.emit('sp', (lambda src=src, xslot=xslot: lambda e: e.dma_start(out=xslot, in_=src))(),
               waits=[("s_xn", max(0, t - 1)), ("s_brow", T_BROW if t < 2 else 0),
                      ("s_browd", T_BROWD if t < 2 else 0)],
               inc=(xsem, 16))
        t_x = p.tot(xsem)
        if not own and t % TPB == 0:
            p.emit('sp', (lambda b=b: lambda e: e.dma_start(
                out=cs_sb[:, 0:1024], in_=cs[:, b * 1024:(b + 1) * 1024]))(),
                waits=[("s_qe", p.tot("s_qe"))], inc=("s_cs", 16))
            p.emit('sp', (lambda b=b: lambda e: e.dma_start(
                out=cs_sb[:, 1024:2048], in_=cs[:, S + b * 1024:S + (b + 1) * 1024]))(),
                inc=("s_cs", 16))
        trickle(6)

        p.emit('dve', (lambda xslot=xslot, sb_=(t % 2) * 16: lambda e: e.reduce_sum(
            stt[:, sb_:sb_ + 1], xslot, axis=AX.X))(),
            waits=[(xsem, t_x), ("s_xn", max(0, t - 1)), ("s_stats", max(0, t - 1)),
                   ("s_pre", PRE_T if t < 2 else 0)],
            inc=("s_dsum", 1))
        xn_sl = xn2[t % 2]
        sb_ = (t % 2) * 16
        p.emit('act', (lambda xslot=xslot, xn_sl=xn_sl, sb_=sb_: lambda e: e.activation(
            xn_sl[:, :], xslot, AF.Square, accum_out=stt[:, sb_ + 1:sb_ + 2]))(),
            waits=[(xsem, t_x), ("s_tp", max(0, 4 * (t - 1))),
                   ("s_browd", T_BROWD if t < 2 else 0),
                   ("s_pre", PRE_T if t < 2 else 0)], inc=("s_sq", 1))

        def stats1(e, b_=sb_):
            _ch("dve", e, e.tensor_scalar_mul(stt[:, b_ + 2:b_ + 3], stt[:, b_:b_ + 1],
                                              1.0 / 4096.0))
            e.tensor_scalar_mul(stt[:, b_ + 3:b_ + 4], stt[:, b_ + 1:b_ + 2], 1.0 / 4096.0)
            _ch("dve", e, e.tensor_mul(stt[:, b_:b_ + 1], stt[:, b_ + 2:b_ + 3],
                                       stt[:, b_ + 2:b_ + 3]))
            return e.tensor_sub(stt[:, b_ + 3:b_ + 4], stt[:, b_ + 3:b_ + 4],
                                stt[:, b_:b_ + 1])
        p.emit('dve', stats1, waits=[("s_sq", t + 1), ("s_dsum", t + 1)], inc=("s_v1", 1))
        p.emit('act', (lambda b_=sb_: lambda e: e.activation(
            stt[:, b_ + 4:b_ + 5], stt[:, b_ + 3:b_ + 4], AF.Sqrt, bias=EPS_AP))(),
               waits=[("s_v1", t + 1)], inc=("s_v2", 1))

        def stats2(e, b_=sb_):
            _ch("dve", e, e.reciprocal(stt[:, b_ + 5:b_ + 6], stt[:, b_ + 4:b_ + 5]))
            return e.tensor_scalar(stt[:, b_ + 6:b_ + 7], stt[:, b_ + 2:b_ + 3],
                                   stt[:, b_ + 5:b_ + 6], -1.0, ALU.mult, ALU.mult)
        p.emit('dve', stats2, waits=[("s_v2", t + 1)], inc=("s_stats", 1))
        p.emit('act', (lambda xslot=xslot, xn_sl=xn_sl, b_=sb_: lambda e: e.activation(
            xn_sl[:, :], xslot, AF.Identity, bias=stt[:, b_ + 6:b_ + 7],
            scale=stt[:, b_ + 5:b_ + 6]))(),
            waits=[("s_stats", t + 1)], inc=("s_xn", 1))

        for beta in range(4):
            tb = 6 + (beta % 2)

            def mk_tp(beta=beta, tb=tb, xn_sl=xn_sl):
                def fn(e):
                    inst = None
                    for j in range(8):
                        k = beta * 8 + j
                        inst = e.matmul(bankb16(tb)[:, j * 128:(j + 1) * 128],
                                        xn_sl[:, k * 128:(k + 1) * 128], identb[:, :],
                                        is_transpose=True,
                                        start=(j == 0), stop=(j == 7))
                    return inst
                return fn
            p.emit('pe', mk_tp(),
                   waits=[("s_xn", t + 1), ("s_pre", PRE_T),
                          ("s_xnTe", max(0, 4 * t + beta - 1)),
                          ("s_ve", t // TPB)],
                   inc=("s_tp", 1))
            tau = t % TPB
            dst = xnT[:, tau * 4096 + beta * 1024: tau * 4096 + (beta + 1) * 1024]
            p.emit('dve', (lambda dst=dst, tb=tb: lambda e: e.tensor_copy(dst, bankb16(tb)))(),
                   waits=[("s_tp", 4 * t + beta + 1)], inc=("s_xnTe", 1))

        if t % TPB == TPB - 1 and not own:
            Plan.label = f"qkv_b{b}" 
            t_xnTe = p.tot("s_xnTe")
            t_cs = p.tot("s_cs")
            for oset in range(2):
                ocs = [0, 1, 2] if oset == 0 else [3, 4, 5]
                q0 = p.tot("s_qe"); v0 = p.tot("s_ve")
                base_wsb = p.tot("s_wsb")
                for k in range(32):
                    slot = k % 2
                    wsem = "s_wst%d" % slot
                    p.emit('sp', (lambda k=k, slot=slot: lambda e: e.dma_start(
                        out=wsb[slot], in_=w_bf[k * 128:(k + 1) * 128, :]))(),
                        waits=[("s_cvout0", cstate["wqkv_out_t"][0] if cstate["wqkv_out_t"] else 0),
                               ("s_cvout1", cstate["wqkv_out_t"][1] if cstate["wqkv_out_t"] else 0),
                               ("s_wsb", max(0, p.tot("s_wsb") - 1))],
                        inc=(wsem, 16))
                    t_wst = p.tot(wsem)

                    def mk_qkv(k=k, slot=slot, ocs=ocs):
                        def fn(e):
                            inst = None
                            for si, oc in enumerate(ocs):
                                lhsT = wsb[slot][:, woff[oc]:woff[oc] + 128]
                                for j in range(2):
                                    inst = e.matmul(bank(si * 2 + j), lhsT, mv_xnT(k, j),
                                                    start=(k == 0), stop=(k == 31))
                            return inst
                        return fn
                    wl = [(wsem, t_wst)]
                    if k == 0:
                        wl += [("s_xnTe", t_xnTe), ("s_qe", q0), ("s_ve", v0),
                               ("s_cs", t_cs)]
                    p.emit('pe', mk_qkv(), waits=wl, inc=("s_wsb", 1))
                t_done = p.tot("s_wsb")
                assert t_done == base_wsb + 32

                for si, oc in enumerate(ocs):
                    Plan.label = f"evac_b{b}_oc{oc}"
                    pw = bank(si * 2, 2)
                    if oc <= 3:
                        h = oc
                        half2 = b >= 2
                        if not half2:
                            dsts = [Qsb[0:64, h * 2048 + b * 1024: h * 2048 + (b + 1) * 1024],
                                    Qsb[64:128, h * 2048 + b * 1024: h * 2048 + (b + 1) * 1024]]
                        else:
                            dsts = [qstage[0:64, :], qstage[64:128, :]]

                        wl = [("s_wsb", t_done), ("s_cs", t_cs),
                              ("s_qe", p.tot("s_qe"))]
                        if half2:
                            wl.append(("s_qsp", p.tot("s_qsp")))
                        p.emit('dve', (lambda pw=pw, h=h: lambda e: e.tensor_scalar_add(
                            qtmp[:, :], pw, bqs[:, h:h + 1]))(),
                            waits=wl, inc=("s_qb", 1))
                        p.emit('pe', (lambda si=si: lambda e: [
                            e.matmul(bank(si * 2), swapm[:, :], qtmp[:, 0:512],
                                     start=True, stop=True),
                            e.matmul(bank(si * 2 + 1), swapm[:, :], qtmp[:, 512:1024],
                                     start=True, stop=True)][-1])(),
                            waits=[("s_qb", p.tot("s_qb"))], inc=("s_qsw", 1))
                        p.emit('dve', (lambda pw=pw, dsts=dsts: _mk_rope(
                            e_qtmp=qtmp, e_swp=pw, e_cs=cs_sb, e_tmp=ropet,
                            dsts=dsts))(),
                            waits=[("s_qsw", p.tot("s_qsw"))], inc=("s_qe", 1))
                        if half2:
                            p.emit('sp', (lambda h=h, b=b: lambda e: e.dma_start(
                                out=q_spill[h * 128:(h + 1) * 128,
                                            (b - 2) * 1024:(b - 1) * 1024],
                                in_=qstage))(),
                                waits=[("s_qe", p.tot("s_qe"))], inc=("s_qsp", 16))
                    elif oc == 4:
                        dsts = [Ksb[0:64, b * 1024:(b + 1) * 1024],
                                Ksb[64:128, b * 1024:(b + 1) * 1024]]

                        p.emit('dve', (lambda pw=pw: lambda e: e.tensor_scalar_add(
                            qtmp[:, :], pw, bks[:, 0:1]))(),
                            waits=[("s_wsb", t_done), ("s_cs", t_cs),
                                   ("s_qe", p.tot("s_qe"))], inc=("s_qb", 1))

                        p.emit('pe', (lambda si=si: lambda e: [
                            e.matmul(bank(si * 2), swapm[:, :], qtmp[:, 0:512],
                                     start=True, stop=True),
                            e.matmul(bank(si * 2 + 1), swapm[:, :], qtmp[:, 512:1024],
                                     start=True, stop=True)][-1])(),
                            waits=[("s_qb", p.tot("s_qb"))], inc=("s_qsw", 1))
                        p.emit('dve', (lambda pw=pw, dsts=dsts: _mk_rope(
                            e_qtmp=qtmp, e_swp=pw, e_cs=cs_sb, e_tmp=ropet,
                            dsts=dsts))(),
                            waits=[("s_qsw", p.tot("s_qsw"))], inc=("s_qe", 1))
                    else:
                        p.emit('act', (lambda pw=pw: lambda e: e.activation(
                            vstage, pw, AF.Identity, bias=bvs[:, 0:1]))(),
                            waits=[("s_wsb", t_done), ("s_vtp", p.tot("s_vtp"))],
                            inc=("s_vs", 1))

                        def mk_vtp():
                            def fn(e):
                                inst = None
                                for j in range(8):
                                    inst = e.matmul(
                                        bankb16(6)[:, j * 128:(j + 1) * 128],
                                        vstage[:, j * 128:(j + 1) * 128], identb[:, :],
                                        is_transpose=True, start=(j == 0), stop=(j == 7))
                                return inst
                            return fn
                        p.emit('pe', mk_vtp(), waits=[("s_vs", p.tot("s_vs")),
                                                      ("s_qe", p.tot("s_qe"))],
                               inc=("s_vtp", 1))
                        p.emit('dve', (lambda b=b: lambda e: e.tensor_copy(
                            Vsb[:, b * 1024:(b + 1) * 1024], bankb16(6)))(),
                            waits=[("s_vtp", p.tot("s_vtp"))], inc=("s_ve", 1))

    while cstate["idx"] < len(conv_jobs):
        emit_conv_job()
    CONV_OUT_T0 = p.tot("s_cvout0"); CONV_OUT_T1 = p.tot("s_cvout1")
    T_XN_ALL = p.tot("s_xn")
    T_QE_ALL = p.tot("s_qe")
    T_VE_ALL = p.tot("s_ve")
    T_QSP_ALL = p.tot("s_qsp")
    T_XNTE_ALL = p.tot("s_xnTe")

    p.emit('sp', lambda e: e.dma_start(out=aT_sb, in_=a_bf[:, :]),
           waits=[("s_cvout0", CONV_OUT_T0), ("s_cvout1", CONV_OUT_T1),
                  ("s_qe", T_QE_ALL), ("s_xn", T_XN_ALL),
                  ("s_qsp", T_QSP_ALL), ("s_cvin0", p.tot("s_cvin0")),
                  ("s_cvin1", p.tot("s_cvin1")), ("s_xnTe", T_XNTE_ALL),
                  ("s_tp", p.tot("s_tp"))],
           inc=("s_wbs", 16))
    p.emit('sp', lambda e: e.dma_start(out=bT_sb, in_=bT_bf[:, :]), inc=("s_wbs", 16))
    p.emit('sp', lambda e: e.dma_start(out=wrT_sb, in_=r_bf[:, :]), inc=("s_wbs", 16))
    p.emit('sp', lambda e: e.dma_start(out=bbr[0:1, 0:DM], in_=biasrows[2:3, :]),
           waits=[("s_browd", T_BROWD)], inc=("s_wbs", 16))
    p.emit('sp', lambda e: e.dma_start(out=bbr[0:1, DM:2 * DM], in_=biasrows[3:4, :]),
           inc=("s_wbs", 16))
    T_FFNW = p.tot("s_wbs")

    def mk_router(e):
        inst = None
        for k in range(32):
            for j in range(2):
                inst = e.matmul(PS[0:8, j * 512:(j + 1) * 512],
                                wrT_sb[:, k * 8:(k + 1) * 8], mv_xnT(k, j),
                                start=(k == 0), stop=(k == 31))
        return inst
    p.emit('pe', mk_router, waits=[("s_wbs", T_FFNW), ("s_xnTe", T_XNTE_ALL),
                                   ("s_qe", T_QE_ALL), ("s_ve", T_VE_ALL)],
           inc=("s_r", 1))
    p.emit('dve', lambda e: e.tensor_copy(ltT[0:8, :], PS[0:8, 0:1024]),
           waits=[("s_r", p.tot("s_r"))], inc=("s_re", 1))

    def mk_ltp(e):
        inst = None
        for tau in range(8):
            inst = e.matmul(bank(2)[0:128, tau * 8:(tau + 1) * 8],
                            ltT[0:8, tau * 128:(tau + 1) * 128], ident8f[:, :],
                            is_transpose=True, start=(tau == 0), stop=(tau == 7))
        return inst
    p.emit('pe', mk_ltp, waits=[("s_re", p.tot("s_re"))], inc=("s_rtp", 1))
    p.emit('dve', lambda e: e.tensor_copy(ltok[:, :], bank(2)[:, 0:64]),
           waits=[("s_rtp", p.tot("s_rtp"))], inc=("s_rte", 1))
    T_RTE = p.tot("s_rte")

    for tau in range(8):
        lt = ltok[:, tau * 8:(tau + 1) * 8]

        def mk_top2a(lt=lt):
            def fn(e):
                _ch("dve", e, e.reduce_max(tp2[:, 0:1], lt, axis=AX.X, negate=True))
                _ch("dve", e, e.tensor_scalar_mul(tp2[:, 1:2], tp2[:, 0:1], -1.0))
                _ch("dve", e, e.tensor_single_scalar(tp2[:, 8:16], lt, tp2[:, 1:2],
                                                     ALU.is_equal))
                _ch("dve", e, e.scalar_tensor_tensor(tp2[:, 16:24], tp2[:, 8:16],
                                                     -1e30, lt, ALU.mult, ALU.add))
                _ch("dve", e, e.reduce_max(tp2[:, 2:3], tp2[:, 16:24], axis=AX.X,
                                           negate=True))
                return e.tensor_scalar_mul(tp2[:, 3:4], tp2[:, 2:3], -1.0)
            return fn
        p.emit('dve', mk_top2a(), waits=[("s_rte", T_RTE), ("s_t2", p.tot("s_t2"))],
               inc=("s_t2", 1))
        p.emit('act', (lambda lt=lt: lambda e: e.activation(
            tp2[:, 24:32], lt, AF.Exp, bias=tp2[:, 0:1]))(),
            waits=[("s_t2", p.tot("s_t2"))], inc=("s_t2", 1))

        def mk_top2b(lt=lt, tau=tau):
            def fn(e):
                _ch("dve", e, e.tensor_single_scalar(tp2[:, 32:40], lt, tp2[:, 3:4],
                                                     ALU.is_ge))
                _ch("dve", e, e.tensor_mul(tp2[:, 24:32], tp2[:, 24:32], tp2[:, 32:40]))
                _ch("dve", e, e.reduce_sum(tp2[:, 4:5], tp2[:, 24:32], axis=AX.X))
                _ch("dve", e, e.reciprocal(tp2[:, 5:6], tp2[:, 4:5]))
                return e.tensor_scalar(cbuf[:, tau * 8:(tau + 1) * 8], tp2[:, 24:32],
                                       tp2[:, 5:6], 2.0, ALU.mult, ALU.mult)
            return fn
        p.emit('dve', mk_top2b(), waits=[("s_t2", p.tot("s_t2"))], inc=("s_t2", 1))
    T_T2 = p.tot("s_t2")

    def mk_ctp(e):
        inst = None
        for tau in range(8):
            bknum = 3 + tau // 4
            inst = e.matmul(PS[0:8, bknum * 512 + (tau % 4) * 128:
                               bknum * 512 + (tau % 4 + 1) * 128],
                            cbuf[:, tau * 8:(tau + 1) * 8], identf[:, :],
                            is_transpose=True, start=(tau % 4 == 0), stop=(tau % 4 == 3))
        return inst
    p.emit('pe', mk_ctp, waits=[("s_t2", T_T2)], inc=("s_ct", 1))
    p.emit('dve', lambda e: e.tensor_copy(ctT[0:8, 0:512], PS[0:8, 3 * 512:4 * 512]),
           waits=[("s_ct", p.tot("s_ct"))], inc=("s_cte", 1))
    p.emit('dve', lambda e: e.tensor_copy(ctT[0:8, 512:1024], PS[0:8, 4 * 512:5 * 512]),
           inc=("s_cte", 1))
    T_CTE = p.tot("s_cte")

    def mk_exp_c(e):
        e.matmul(bank(5), Ef[:, :], ctT[0:8, 0:512], start=True, stop=True)
        return e.matmul(bank(2), Ef[:, :], ctT[0:8, 512:1024], start=True, stop=True)
    p.emit('pe', mk_exp_c, waits=[("s_cte", T_CTE), ("s_rte", T_RTE)], inc=("s_cx", 1))
    p.emit('dve', lambda e: e.tensor_copy(ce[:, 0:512], bank(5)),
           waits=[("s_cx", p.tot("s_cx"))], inc=("s_cxe", 1))
    p.emit('dve', lambda e: e.tensor_copy(ce[:, 512:1024], bank(2)), inc=("s_cxe", 1))
    T_CXE = p.tot("s_cxe")

    def mk_h(e):
        inst = None
        for k in range(32):
            for j in range(2):
                inst = e.matmul(bank(6 + j), aT_sb[:, k * 128:(k + 1) * 128],
                                mv_xnT(k, j), start=(k == 0), stop=(k == 31))
        return inst
    p.emit('pe', mk_h, waits=[("s_wbs", T_FFNW), ("s_xnTe", T_XNTE_ALL),
                              ("s_ve", T_VE_ALL)], inc=("s_h", 1))
    p.emit('dve', lambda e: e.tensor_mul(hw[:, :], bank(6, 2), ce[:, :]),
           waits=[("s_h", p.tot("s_h")), ("s_cxe", T_CXE)], inc=("s_hw", 1))
    T_HW = p.tot("s_hw")

    for tp_i in range(4):
        for half in range(2):
            for k in range(33):
                if k < 32:
                    slot = k % 3
                    wsem = "s_wst%d" % slot
                    p.emit('sp', (lambda k=k, half=half, slot=slot: lambda e: e.dma_start(
                        out=wbstream[slot],
                        in_=wb_bf[k * 128:(k + 1) * 128, half * 2048:(half + 1) * 2048]))(),
                        waits=[("s_base", p.tot("s_base") - (0 if k < 3 else 2)),
                               ("s_cvout0", CONV_OUT_T0),
                               ("s_cvout1", CONV_OUT_T1)],
                        inc=(wsem, 16))
                    t_w = p.tot(wsem)

                def mk_base(k=k, tp_i=tp_i, half=half):
                    def fn(e):
                        inst = None
                        for ti in range(2):
                            tau = tp_i * 2 + ti
                            if k < 32:
                                lhsT = xnT[:, tau * 4096 + k * 128:
                                           tau * 4096 + (k + 1) * 128]
                                mvr = wbstream[k % 3]
                            else:
                                lhsT = hw[:, tau * 128:(tau + 1) * 128]
                                mvr = bT_sb[:, half * 2048:(half + 1) * 2048]
                            for j in range(4):
                                inst = e.matmul(bank(ti * 4 + j), lhsT,
                                                mvr[:, j * 512:(j + 1) * 512],
                                                start=(k == 0), stop=False)
                        return inst
                    return fn
                wl = [(wsem, t_w)] if k < 32 else [("s_hw", T_HW)]
                if k == 0:
                    wl += [("s_hw", T_HW), ("s_cxe", T_CXE), ("s_cte", T_CTE),
                           ("s_re", p.tot("s_re")), ("s_rte", T_RTE),
                           ("s_g", p.tot("s_g"))]
                p.emit('pe', mk_base(), waits=wl,
                       inc=("s_base", 1) if k < 32 else None)

            def mk_bbias():
                def fn(e):
                    inst = None
                    for ti in range(2):
                        for j in range(4):
                            col = half * 2048 + j * 512
                            e.matmul(bank(ti * 4 + j), onesrow_b[:, :],
                                     bbr[0:1, col:col + 512], start=False, stop=False)
                            inst = e.matmul(bank(ti * 4 + j), onesrow_b[:, :],
                                            bbr[0:1, DM + col:DM + col + 512],
                                            start=False, stop=True)
                    return inst
                return fn
            p.emit('pe', mk_bbias(), waits=[("s_wbs", T_FFNW)], inc=("s_bmm", 1))
            t_bs = p.tot("s_bmm")
            for ti in range(2):
                tau = tp_i * 2 + ti
                gslot = gstage[ti]
                gsem = "s_gw%d" % ti
                p.emit('act', (lambda ti=ti, gslot=gslot: lambda e: e.activation(
                    gslot, bank(ti * 4, 4), AF.Silu))(),
                    waits=[("s_bmm", t_bs), (gsem, p.tot(gsem))],
                    inc=("s_g", 1))
                p.emit('gp', (lambda tau=tau, half=half, gslot=gslot: lambda e: e.dma_start(
                    out=g_dram[tau * 128:(tau + 1) * 128, half * 2048:(half + 1) * 2048],
                    in_=gslot))(),
                    waits=[("s_g", p.tot("s_g"))], inc=(gsem, 16))
    T_G_ALL0 = p.tot("s_gw0"); T_G_ALL1 = p.tot("s_gw1")
    T_SILU = p.tot("s_g")
    T_BMM = p.tot("s_bmm")

    if DBG:
        p.emit('gp', lambda e: e.dma_start(out=dbg["wbf"], in_=w_bf),
               waits=[("s_cvout0", CONV_OUT_T0), ("s_cvout1", CONV_OUT_T1)],
               inc=("s_dbg", 16))
        p.emit('gp', lambda e: e.dma_start(out=dbg["abf"], in_=a_bf),
               waits=[("s_cvout0", CONV_OUT_T0), ("s_cvout1", CONV_OUT_T1)],
               inc=("s_dbg", 16))
        p.emit('gp', lambda e: e.dma_start(out=dbg["xnT"], in_=xnT),
               waits=[("s_xnTe", T_XNTE_ALL)], inc=("s_dbg", 16))
        p.emit('gp', lambda e: e.dma_start(out=dbg["Q"], in_=Qsb[:, :]),
               waits=[("s_qe", T_QE_ALL)], inc=("s_dbg", 16))
        p.emit('gp', lambda e: e.dma_start(out=dbg["K"], in_=Ksb[:, :]),
               inc=("s_dbg", 16))
        p.emit('gp', lambda e: e.dma_start(out=dbg["V"], in_=Vsb[:, :]),
               waits=[("s_ve", T_VE_ALL)], inc=("s_dbg", 16))
        p.emit('gp', lambda e: e.dma_start(out=dbg["hw"], in_=hw[:, :]),
               waits=[("s_hw", T_HW)], inc=("s_dbg", 16))
        p.emit('gp', lambda e: e.dma_start(out=dbg["ce"], in_=ce[:, :]),
               waits=[("s_cxe", T_CXE)], inc=("s_dbg", 16))
        p.emit('gp', lambda e: e.dma_start(out=dbg["lt"], in_=ltok[:, :]),
               waits=[("s_rte", T_RTE)], inc=("s_dbg", 16))

    PHASE1_ONLY = os.environ.get("KMK_PHASE") == "1"
    if not PHASE1_ONLY:
        p.emit('sp', lambda e: e.dma_start(out=mask_sb, in_=masks_d),
               waits=[("s_g", T_SILU)], inc=("s_msk0", 16))

        def pt_slice(hi, slot):
            c = (hi * 2 + slot) * 512
            return PTs[:, c:c + 512]

        for qq in ([] if PHASE1_ONLY else range(4)):
            if qq >= 1:
                t_stmm_q = p.tot("s_stmm")
                for h in range(4):
                    p.emit('sp', (lambda h=h, qq=qq: lambda e: e.dma_start(
                        out=Qsb[:, h * 2048:(h + 1) * 2048],
                        in_=q_spill[h * 128:(h + 1) * 128,
                                    (qq - 1) * 2048:qq * 2048]))(),
                        waits=[("s_stmm", t_stmm_q), ("s_qsp", T_QSP_ALL),
                               ("s_dbg", p.tot("s_dbg"))],
                        inc=("s_q2", 16))
            T_Q2 = p.tot("s_q2")
            for lqb in range(4):
                qb = qq * 4 + lqb
                kcmax = 4 * (qb + 1)
                qcol = lqb * 512
                for hp in range(2):
                    heads = (hp * 2, hp * 2 + 1)
                    base_cmm = p.tot("s_cmm")
                    base_stmm = p.tot("s_stmm")
                    base_exp = p.tot("s_exp")
                    base_dp = p.tot("s_dp")
                    for it in range(kcmax + 1):
                        if it < kcmax:
                            kc = it
                            slot = kc % 2
                            sb0 = 2 + slot * 2

                            def mk_st(kc=kc, sb0=sb0, heads=heads, qcol=qcol):
                                def fn(e):
                                    inst = None
                                    for i, h in enumerate(heads):
                                        inst = e.matmul(
                                            bank(sb0 + i),
                                            Ksb[:, kc * 128:(kc + 1) * 128],
                                            Qsb[:, h * 2048 + qcol: h * 2048 + qcol + 512],
                                            start=True, stop=True)
                                    return inst
                                return fn
                            expw = base_exp + (kc - 1) if kc >= 2 else base_exp
                            wl = [("s_exp", max(0, expw))]
                            if qb == 0 and hp == 0 and kc == 0:
                                wl += [("s_xn", T_XN_ALL), ("s_qe", T_QE_ALL),
                                       ("s_ve", T_VE_ALL), ("s_msk0", 16),
                                       ("s_bmm", T_BMM), ("s_g", T_SILU)]
                            if qq >= 1 and lqb == 0 and hp == 0 and kc == 0:
                                wl.append(("s_q2", T_Q2))
                            p.emit('pe', mk_st(), waits=wl, inc=("s_stmm", 1))
                        if it >= 1:
                            kcc = it - 1
                            slot = kcc % 2
                            sb0 = 2 + slot * 2
                            diag = kcc >= kcmax - 4
                            expwl = [("s_stmm", base_stmm + kcc + 1)]
                            if kcc >= 2:
                                expwl += [("s_cmm", base_cmm + kcc - 1),
                                          ("s_dp", base_dp + kcc // 2)]
                            p.emit('act', (lambda sb0=sb0, slot=slot:
                                           lambda e: e.activation(
                                               PTs[:, slot * 1024:(slot + 1) * 1024],
                                               bank(sb0, 2), AF.Exp,
                                               bias=SHIFT_AP, scale=GAMMA_AP))(),
                                   waits=expwl, inc=("s_exp", 1))
                            t_exp = p.tot("s_exp")
                            if diag:
                                c = kcc - (kcmax - 4)
                                p.emit('dve', (lambda slot=slot, c=c: lambda e:
                                               e.tensor_mul(
                                                   PTs[:, slot * 1024:(slot + 1) * 1024],
                                                   PTs[:, slot * 1024:(slot + 1) * 1024],
                                                   mask_sb[:, c * 1024:(c + 1) * 1024]))(),
                                       waits=[("s_exp", t_exp)], inc=("s_msk", 1))
                            t_msk = p.tot("s_msk")

                            def mk_ctx(kcc=kcc, slot=slot, kcmax=kcmax):
                                def fn(e):
                                    inst = None
                                    st = (kcc == 0); ed = (kcc == kcmax - 1)
                                    for i in range(2):
                                        inst = e.matmul(
                                            bank(i), Vsb[:, kcc * 128:(kcc + 1) * 128],
                                            PTs[:, slot * 1024 + i * 512:
                                                slot * 1024 + (i + 1) * 512],
                                            start=st, stop=ed)
                                    return inst
                                return fn
                            wl = [("s_exp", t_exp)]
                            if diag:
                                wl.append(("s_msk", t_msk))
                            if kcc == 0:
                                wl.append(("s_cn", p.tot("s_cn")))
                            p.emit('pe', mk_ctx(), waits=wl, inc=("s_cmm", 1))
                            if kcc % 2 == 1:
                                p.emit('dve', lambda e: e.tensor_add(
                                    dtmp[:, :], PTs[:, 0:1024], PTs[:, 1024:2048]),
                                    waits=[("s_exp", t_exp),
                                           ("s_msk", p.tot("s_msk")),
                                           ("s_dmm", p.tot("s_dmm"))],
                                    inc=("s_dp", 1))

                                def mk_dmm(kcc=kcc, kcmax=kcmax):
                                    def fn(e):
                                        st = (kcc == 1); ed = (kcc == kcmax - 1)
                                        e.matmul(PS[0:1, 6 * 512:7 * 512],
                                                 onescol[:, :], dtmp[:, 0:512],
                                                 start=st, stop=ed)
                                        return e.matmul(PS[0:1, 7 * 512:8 * 512],
                                                        onescol[:, :], dtmp[:, 512:1024],
                                                        start=st, stop=ed)
                                    return fn
                                p.emit('pe', mk_dmm(),
                                       waits=[("s_dp", p.tot("s_dp"))],
                                       inc=("s_dmm", 1))
                    t_cmm = p.tot("s_cmm")
                    t_dmm = p.tot("s_dmm")

                    def mk_dev(e):
                        e.tensor_copy(scr2[0:1, 0:512], PS[0:1, 6 * 512:7 * 512])
                        _ch("dve", e, e.tensor_copy(scr2[0:1, 512:1024],
                                                    PS[0:1, 7 * 512:8 * 512]))
                        e.reciprocal(scr2[0:1, 1024:1536], scr2[0:1, 0:512])
                        return e.reciprocal(scr2[0:1, 1536:2048], scr2[0:1, 512:1024])
                    p.emit('dve', mk_dev, waits=[("s_cmm", t_cmm), ("s_dmm", t_dmm)],
                           inc=("s_rec", 1))
                    t_rec = p.tot("s_rec")
                    for i, h in enumerate(heads):
                        p.emit('pe', (lambda i=i: lambda e: e.matmul(
                            bank(7), onesrow_f[:, :],
                            scr2[0:1, 1024 + i * 512:1536 + i * 512],
                            start=True, stop=True))(),
                            waits=[("s_rec", t_rec), ("s_rbe", p.tot("s_rbe"))],
                            inc=("s_bc", 1))
                        p.emit('dve', lambda e: e.tensor_copy(rb[:, :], bank(7)),
                               waits=[("s_bc", p.tot("s_bc")),
                                      ("s_cn", p.tot("s_cn"))], inc=("s_rbe", 1))
                        p.emit('dve', (lambda i=i: lambda e: e.tensor_mul(
                            ctxn[:, :], bank(i), rb[:, :]))(),
                            waits=[("s_rbe", p.tot("s_rbe")), ("s_send", p.tot("s_send"))],
                            inc=("s_cn", 1))
                        p.emit('gp', (lambda h=h, qb=qb: lambda e: e.dma_start(
                            out=ctx_send[h * 128:(h + 1) * 128, qb * 512:(qb + 1) * 512],
                            in_=ctxn[:, :]))(),
                            waits=[("s_cn", p.tot("s_cn"))], inc=("s_send", 16))
        T_SEND = p.tot("s_send")

        if not PHASE1_ONLY:
            p.emit('gp', lambda e: e.collective_compute(
                "AllGather", ALU.bypass, replica_groups=[list(range(NC))],
                ins=[ctx_send.opt()], outs=[gath.opt()]),
                waits=[("s_send", T_SEND)], inc=("s_cc", 1))


        def mk_gathread(e):
            pid = e.partition_id()
            g3 = gath.rearrange("(k p) (n t) -> n p k t", p=128, t=1024)
            return e.dma_start(out=ctxr, in_=g3[pid])
        p.emit('sp', mk_gathread, waits=[("s_cc", 1), ("s_gw0", T_G_ALL0),
                                     ("s_gw1", T_G_ALL1)], inc=("s_ctxr", 16))
        p.emit('sp', lambda e: e.dma_start(out=bor[0:1, 0:DM], in_=biasrows[0:1, :]),
               inc=("s_ctxr", 16))
        p.emit('sp', lambda e: e.dma_start(out=bor[0:1, DM:2 * DM], in_=biasrows[1:2, :]),
               inc=("s_ctxr", 16))
        T_CTXR = p.tot("s_ctxr")

        for tp_i in range(4):
            for half in range(2):
                for k in range(32):
                    slot = k % 4
                    wsem = "s_wst%d" % slot
                    p.emit('sp', (lambda k=k, half=half, slot=slot: lambda e: e.dma_start(
                        out=wostream[slot],
                        in_=wo_bf[k * 128:(k + 1) * 128, half * 2048:(half + 1) * 2048]))(),
                        waits=[("s_wmm", p.tot("s_wmm") - (0 if k < 4 else 3)),
                               ("s_cvout0", CONV_OUT_T0),
                               ("s_cvout1", CONV_OUT_T1)],
                        inc=(wsem, 16))
                    t_w = p.tot(wsem)

                    def mk_wo(k=k, tp_i=tp_i, slot=slot):
                        def fn(e):
                            inst = None
                            for ti in range(2):
                                tau = tp_i * 2 + ti
                                lhsT = ctxr[:, k * 1024 + tau * 128: k * 1024 + (tau + 1) * 128]
                                for j in range(4):
                                    inst = e.matmul(bank(ti * 4 + j), lhsT,
                                                    wostream[slot][:, j * 512:(j + 1) * 512],
                                                    start=(k == 0), stop=False)
                            return inst
                        return fn
                    wl = [(wsem, t_w)]
                    if k == 0:
                        wl += [("s_ctxr", T_CTXR), ("s_ev1", p.tot("s_ev1"))]
                    p.emit('pe', mk_wo(), waits=wl, inc=("s_wmm", 1))

                def mk_wobias(half=half):
                    def fn(e):
                        inst = None
                        for ti in range(2):
                            for j in range(4):
                                col = half * 2048 + j * 512
                                e.matmul(bank(ti * 4 + j), onesrow_b[:, :],
                                         bor[0:1, col:col + 512], start=False, stop=False)
                                inst = e.matmul(bank(ti * 4 + j), onesrow_b[:, :],
                                                bor[0:1, DM + col:DM + col + 512],
                                                start=False, stop=True)
                        return inst
                    return fn
                p.emit('pe', mk_wobias(), waits=[("s_ctxr", T_CTXR)], inc=("s_bmm", 1))
                t_wod = p.tot("s_bmm")
                for ti in range(2):
                    tau = tp_i * 2 + ti
                    p.emit('gp', (lambda tau=tau, half=half, ti=ti: lambda e: e.dma_start(
                        out=ghalf[ti], in_=g_dram[tau * 128:(tau + 1) * 128,
                                                  half * 2048:(half + 1) * 2048]))(),
                        waits=[("s_gw0", T_G_ALL0), ("s_gw1", T_G_ALL1),
                               ("s_ev1", p.tot("s_ev1"))],
                        inc=("s_gh", 16))
                    p.emit('sp', (lambda tau=tau, half=half, ti=ti: lambda e: e.dma_start(
                        out=xohalf[ti], in_=xown[tau * 128:(tau + 1) * 128,
                                                 half * 2048:(half + 1) * 2048]))(),
                        waits=[("s_ev1", p.tot("s_ev1"))],
                        inc=("s_xo", 16))
                    t_gh = p.tot("s_gh"); t_xo = p.tot("s_xo")

                    def mk_fin(ti=ti):
                        def fn(e):
                            _ch("dve", e, e.tensor_mul(ohalf[ti], bank(ti * 4, 4),
                                                       ghalf[ti]))
                            return e.tensor_add(ohalf[ti], ohalf[ti], xohalf[ti])
                        return fn
                    p.emit('dve', mk_fin(), waits=[("s_bmm", t_wod), ("s_gh", t_gh),
                                                   ("s_xo", t_xo),
                                                   ("s_out", p.tot("s_out"))],
                           inc=("s_ev1", 1))
                    p.emit('gp', (lambda tau=tau, half=half, ti=ti: lambda e: e.dma_start(
                        out=out_ext[tau * 128:(tau + 1) * 128,
                                    half * 2048:(half + 1) * 2048],
                        in_=ohalf[ti]))(),
                        waits=[("s_ev1", p.tot("s_ev1"))], inc=("s_out", 16))
        if DBG:
            p.emit('gp', lambda e: e.dma_start(out=dbg["scr2"], in_=scr2[:, :]),
                   waits=[("s_cc", 1)], inc=("s_dbg", 16))
            p.emit('gp', lambda e: e.dma_start(out=dbg["send"], in_=ctx_send),
                   waits=[("s_cc", 1)], inc=("s_dbg", 16))
            p.emit('gp', lambda e: e.dma_start(out=dbg["ctxr"], in_=ctxr[:, :]),
                   waits=[("s_ctxr", T_CTXR)], inc=("s_dbg", 16))
        T_OUT = p.tot("s_out")
        p.emit('gp', None, waits=[("s_out", T_OUT), ("s_dbg", p.tot("s_dbg"))])

    else:
        p.emit('gp', lambda e: e.dma_start(out=out_ext[0:128, :],
                                           in_=Qsb[:, 0:8192].bitcast(F32)),
               waits=[("s_qe", T_QE_ALL)], inc=("s_out", 16))
        p.emit('gp', None, waits=[("s_out", p.tot("s_out")),
                                  ("s_dbg", p.tot("s_dbg")),
                                  ("s_cvout0", CONV_OUT_T0),
                                  ("s_cvout1", CONV_OUT_T1),
                                  ("s_gw0", T_G_ALL0), ("s_gw1", T_G_ALL1)])

    semnames = sorted({nm for (_, ws, _, inc, _t) in p.ops
                       for nm in ([w[0] for w in ws] + ([inc[0]] if inc else []))})
    import contextlib
    stack = contextlib.ExitStack()
    sems = {}

    with nc.Block() as block:
        for sn in semnames:
            sems[sn] = stack.enter_context(nc.semaphore(sn))
        for key in ("dve", "act", "gp"):
            _CHSEMS[key] = stack.enter_context(nc.semaphore("s_ch_" + key))
        _CH.update({"dve": 0, "act": 0, "gp": 0})

        def runner(engine_key):
            def run(eng):
                for (ek, ws, fn, inc, _tag) in p.ops:
                    if ek != engine_key:
                        continue
                    for (sn, v) in ws:
                        eng.wait_ge(sems[sn], v)
                    if fn is None:
                        continue
                    inst = fn(eng)
                    if inc is not None:
                        assert inst is not None
                        inst.then_inc(sems[inc[0]], inc[1])
            return run

        block.tensor(runner('pe'))
        block.scalar(runner('act'))
        block.vector(runner('dve'))
        block.gpsimd(runner('gp'))
        block.sync(runner('sp'))

    if return_plan:
        return nc, p
    return nc


_NC_CACHE = {}


def _host_inputs(inputs):
    x = np.ascontiguousarray(np.asarray(inputs["L0_attn_hidden_states"],
                                        dtype=np.float32).reshape(S, DM))
    Wq = np.asarray(inputs["L0_attn_Wq"], np.float32)
    bq = np.asarray(inputs["L0_attn_bq"], np.float32)
    Wk = np.asarray(inputs["L0_attn_Wk"], np.float32)
    bk = np.asarray(inputs["L0_attn_bk"], np.float32)
    Wv = np.asarray(inputs["L0_attn_Wv"], np.float32)
    bv = np.asarray(inputs["L0_attn_bv"], np.float32)
    Wo = np.ascontiguousarray(np.asarray(inputs["L0_attn_Wo"], np.float32))
    bo_ = np.asarray(inputs["L0_attn_bo"], np.float32)
    Wb = np.asarray(inputs["L0_ffn_W_base"], np.float32)
    bb_ = np.asarray(inputs["L0_ffn_b_base"], np.float32)
    Wr = np.asarray(inputs["L0_ffn_W_router"], np.float32)
    A = np.asarray(inputs["L0_ffn_A"], np.float32)
    B = np.asarray(inputs["L0_ffn_B"], np.float32)

    j = np.arange(64, dtype=np.float32) / np.float32(64.0)
    inv = (np.float32(1.0) / np.power(np.float32(10000.0), j)).astype(np.float32)
    pos = np.arange(S, dtype=np.float32)
    ang = (pos[:, None] * inv[None, :]).astype(np.float32)
    cosT = np.cos(ang).T.astype(np.float32)
    sinT = np.sin(ang).T.astype(np.float32)
    csA = np.concatenate([cosT, sinT], axis=0)
    csB = np.concatenate([sinT, cosT], axis=0)
    cs_tab = np.ascontiguousarray(np.concatenate([csA, csB], axis=1).astype(np.float32))
    swapm_np = np.zeros((128, 128), np.float32)
    for i in range(128):
        swapm_np[i, (i + 64) % 128] = 1.0

    mask_np = np.zeros((128, 4 * 1024), dtype=ml_dtypes.bfloat16)
    for c in range(4):
        kk = np.arange(128)[:, None] + 128 * c
        qq = np.arange(512)[None, :]
        mc = (kk <= qq).astype(ml_dtypes.bfloat16)
        mask_np[:, c * 1024:c * 1024 + 512] = mc
        mask_np[:, c * 1024 + 512:(c + 1) * 1024] = mc

    Ef_np = np.zeros((8, 128), np.float32)
    for e in range(8):
        Ef_np[e, e * 16:(e + 1) * 16] = 1.0
    cvec_np = np.zeros((128, 4), np.float32)
    cvec_np[:, 0] = SHIFT; cvec_np[:, 1] = GAMMA; cvec_np[:, 2] = EPS

    common = dict(
        x=x, wo=Wo,
        bo=np.ascontiguousarray(bo_.reshape(1, DM)),
        wbT=np.ascontiguousarray(Wb.T),
        bb=np.ascontiguousarray(bb_.reshape(1, DM)),
        wrT=np.ascontiguousarray(Wr.T),
        aT=np.ascontiguousarray(A.reshape(128, DM).T),
        bTm=np.ascontiguousarray(B.transpose(0, 2, 1).reshape(128, DM)),
        cs=cs_tab, masks=mask_np,
        identb=np.eye(128, dtype=ml_dtypes.bfloat16),
        identf=np.eye(128, dtype=np.float32),
        ident8f=np.eye(8, dtype=np.float32),
        onescol=np.ones((128, 1), ml_dtypes.bfloat16),
        onesrow_b=np.ones((1, 128), ml_dtypes.bfloat16),
        onesrow_f=np.ones((1, 128), np.float32),
        Ef=Ef_np, cvec=cvec_np, swapm=swapm_np,
    )
    in_maps = []
    for g in range(NC):
        m = dict(common)
        m.update(
            xown=np.ascontiguousarray(x[g * OWN:(g + 1) * OWN, :]),
            wq=np.ascontiguousarray(Wq[:, g * 512:(g + 1) * 512]),
            wk=np.ascontiguousarray(Wk[:, g * 128:(g + 1) * 128]),
            wv=np.ascontiguousarray(Wv[:, g * 128:(g + 1) * 128]),
            bq4=np.ascontiguousarray(bq[g * 512:(g + 1) * 512].reshape(4, 128).T),
            bk1=np.ascontiguousarray(bk[g * 128:(g + 1) * 128].reshape(1, 128).T),
            bv1=np.ascontiguousarray(bv[g * 128:(g + 1) * 128].reshape(1, 128).T),
        )
        in_maps.append(m)
    return in_maps


def kernel(**inputs):
    in_maps = _host_inputs(inputs)
    if "nc" not in _NC_CACHE:
        _NC_CACHE["nc"] = build_program()
    nc = _NC_CACHE["nc"]
    res = run_bass_kernel_spmd(nc, in_maps, core_ids=list(range(NC)))
    out = np.concatenate([res.results[g]["out"] for g in range(NC)], axis=0)
    return np.ascontiguousarray(out.reshape(1, S, DM))
